# version 02
# baseline (speedup 1.0000x reference)
import numpy as np
import ml_dtypes
import concourse.bass as bass
import concourse.mybir as mybir
from concourse.bass_utils import run_bass_kernel_spmd
from contextlib import ExitStack

F32 = mybir.dt.float32
BF16 = mybir.dt.bfloat16
I32 = mybir.dt.int32
U16 = mybir.dt.uint16
U8 = mybir.dt.uint8
AF = mybir.ActivationFunctionType
ALU = mybir.AluOpType
AX = mybir.AxisListType

NCORE = 8
T = 2048
D = 2048
NEG = np.float32(-3.4028235e38)
ENGS = ("sync", "tensor", "vector", "scalar", "gpsimd")
SBUF_BYTES = 207 * 1024



def _bcast_chunks(v, p=128):
    return np.ascontiguousarray(np.asarray(v).reshape(-1, p).T)


def prep_host(inputs):
    i = {k: np.asarray(v) for k, v in inputs.items()}
    H = 16
    d = {}
    x = i["L0_attn_hidden_states"]
    d["_xcores"] = [np.ascontiguousarray(x[c].T) for c in range(NCORE)]

    d["wqa"] = np.ascontiguousarray(i["L0_attn_Wq_a"])
    d["qnw"] = _bcast_chunks(i["L0_attn_q_norm_w"])
    wqb = i["L0_attn_Wq_b"]
    perm = []
    for h in range(H):
        perm.extend(h * 192 + j for j in range(128))
    for g in range(4):
        for half in range(2):
            for hh in range(4):
                perm.extend((4 * g + hh) * 192 + 128 + half * 32 + j for j in range(32))
    d["wqb"] = np.ascontiguousarray(wqb[:, perm])
    d["wkr"] = np.ascontiguousarray(i["L0_attn_Wk_rope"])
    d["wkva"] = np.ascontiguousarray(i["L0_attn_Wkv_a"])
    d["kvnw"] = _bcast_chunks(i["L0_attn_kv_norm_w"])
    wkvb = i["L0_attn_Wkv_b"]
    perm2 = [h * 256 + j for h in range(H) for j in range(128)] + \
            [h * 256 + 128 + j for h in range(H) for j in range(128)]
    d["wkvb"] = np.ascontiguousarray(wkvb[:, perm2])
    d["wo0"] = np.ascontiguousarray(i["L0_attn_Wo"])

    d["w1t"] = np.ascontiguousarray(i["L0_ffn_W1"].T)
    d["b1c"] = _bcast_chunks(i["L0_ffn_b1"])
    d["w2t"] = np.ascontiguousarray(i["L0_ffn_W2"].T)
    d["b2c"] = _bcast_chunks(i["L0_ffn_b2"])
    d["w3t"] = np.ascontiguousarray(i["L0_ffn_W3"].T)
    d["b3c"] = _bcast_chunks(i["L0_ffn_b3"])

    d["wq1"] = np.ascontiguousarray(i["L1_attn_Wq"].T)
    d["wk1"] = np.ascontiguousarray(i["L1_attn_Wk"].T)
    d["wv1"] = np.ascontiguousarray(i["L1_attn_Wv"].T)
    d["wo1"] = np.ascontiguousarray(i["L1_attn_Wo"].T)

    d["gwp"] = np.ascontiguousarray(i["L1_ffn_gate_W_packed"].T)
    d["gsc"] = np.ascontiguousarray(i["L1_ffn_gate_scales"].T)
    d["uwp"] = np.ascontiguousarray(i["L1_ffn_up_W_packed"].T)
    d["usc"] = np.ascontiguousarray(i["L1_ffn_up_scales"].T)
    d["dwp"] = np.ascontiguousarray(i["L1_ffn_down_W_packed"].T)
    d["dsc"] = np.ascontiguousarray(i["L1_ffn_down_scales"].T)

    inv = 1.0 / 10000.0 ** (np.arange(0, 64, 2, dtype=np.float32) / 64)
    tt = np.arange(T, dtype=np.float32)
    fr = np.outer(tt, inv)
    cos = np.cos(fr).astype(np.float32).T
    sin = np.sin(fr).astype(np.float32).T
    d["cos4"] = np.ascontiguousarray(np.tile(cos, (4, 1)))
    d["sin4"] = np.ascontiguousarray(np.tile(sin, (4, 1)))
    d["cos4b"] = d["cos4"].astype(ml_dtypes.bfloat16)
    d["sin4b"] = d["sin4"].astype(ml_dtypes.bfloat16)
    ii = np.arange(128)
    d["mdiag"] = np.where(ii[None, :] > ii[:, None], NEG, np.float32(0.0)).astype(np.float32)
    m1 = np.zeros((4, 128, 512), np.float32)
    for oi, delta in enumerate((0, 128, 256, 384)):
        jj = np.arange(512)
        m1[oi] = (jj[None, :] >= (ii[:, None] + delta)).astype(np.float32)
    d["m1s"] = np.ascontiguousarray(m1.astype(ml_dtypes.bfloat16))
    return d



class Prog:

    def __init__(self, active, eng, sempool):
        self.active = active
        self._eng = eng
        self.pool = sempool
        self.idx = {}
        self.cnt = {}

    def _sem(self, name):
        if name not in self.idx:
            i = len(self.idx)
            assert i < len(self.pool), "semaphore pool exhausted"
            self.idx[name] = i
            self.cnt[name] = 0
        return self.pool[self.idx[name]]

    def e(self, name):
        return self._eng.get(name) if name == self.active else None

    def inc(self, instr, sem, amt):
        h = self._sem(sem)
        self.cnt[sem] += amt
        if instr is not None:
            instr.then_inc(h, amt)
        return self.cnt[sem]

    def dinc(self, engname, sem, amt=1):
        h = self._sem(sem)
        self.cnt[sem] += amt
        e = self.e(engname)
        if e is not None:
            e.drain().then_inc(h, amt)
        return self.cnt[sem]

    def fence(self, engname, sem):
        v = self.dinc(engname, sem)
        e = self.e(engname)
        if e is not None:
            e.wait_ge(self._sem(sem), v)
        return (sem, v)

    def tok(self, sem):
        self._sem(sem)
        return (sem, self.cnt[sem])

    def wait(self, engname, token):
        if token is None:
            return
        if isinstance(token, list):
            for t in token:
                self.wait(engname, t)
            return
        sem, val = token
        h = self._sem(sem)
        e = self.e(engname)
        if e is not None and val > 0:
            e.wait_ge(h, val)


class Carver:
    def __init__(self, backing_ap, nbytes):
        self.b = backing_ap
        self.n = nbytes

    def at(self, off, p, f, dt):
        sz = f * mybir.dt.size(dt)
        assert off % 4 == 0 and off + sz <= self.n, (off, sz, self.n)
        return self.b[0:p, off:off + sz].bitcast(dt)


NSEMS = 96


def build_engines(nc, emit, sbuf_bytes=SBUF_BYTES):
    with ExitStack() as ctx:
        backing = ctx.enter_context(nc.sbuf_tensor("arena", [128, sbuf_bytes], U8))
        C = Carver(backing.ap(), sbuf_bytes)
        psum = [ctx.enter_context(nc.psum_tensor(f"pb{i}", [128, 512], F32))
                for i in range(8)]
        pool = [ctx.enter_context(nc.semaphore(f"s{i}")) for i in range(NSEMS)]
        block = ctx.enter_context(nc.Block())
        eng = {}

        def section(engname):
            def run(e):
                eng[engname] = e
                p = Prog(engname, eng, pool)
                emit(p, C, psum)
            return run

        block.sync(section("sync"))
        block.tensor(section("tensor"))
        block.vector(section("vector"))
        block.scalar(section("scalar"))
        block.gpsimd(section("gpsimd"))
    return nc



def emit_ln(p, C, psum, *, x_dram, rows_dram, out_cb, lay, tag, upto='full', dbg_dram=None):
    xt = [C.at(lay["xt"] + k * 8192, 128, 2048, F32) for k in range(2)]
    sq = [C.at(lay["sq"] + k * 8192, 128, 2048, F32) for k in range(2)]
    mB = C.at(lay["mB"], 128, 2048, F32)
    rB = C.at(lay["rB"], 128, 2048, F32)
    s16 = C.at(lay["s16"], 128, 64, F32)
    mean16 = s16[:, 0:16]; var16 = s16[:, 16:32]; sv16 = s16[:, 32:48]; rs16 = s16[:, 48:64]
    ones_col = C.at(lay["ones"], 128, 1, F32)
    xsem = [tag + "x0", tag + "x1"]

    ps_cols = psum[6]
    sred = C.at(lay["s16"] + 256, 128, 32, F32)
    v = p.e("vector")
    _ = v and v.memset(ones_col, 1.0)
    ones_tok = ("vd", p.dinc("vector", "vd"))

    xt_free = [None, None]
    sq_free = [None, None]
    tmm = 0
    for k in range(16):
        b = k % 2
        g = p.e("gpsimd")
        p.wait("gpsimd", xt_free[b])
        tload = p.inc(g and g.dma_start(out=xt[b], in_=x_dram[128 * k:128 * (k + 1), :]),
                      xsem[b], 16)
        p.wait("vector", (xsem[b], tload))
        p.wait("vector", sq_free[b])
        vv = p.e("vector")
        _ = vv and vv.tensor_mul(sq[b], xt[b], xt[b])
        tsq = p.dinc("vector", "vd")
        t = p.e("tensor")
        p.wait("tensor", (xsem[b], tload))
        p.wait("tensor", ("vd", tsq))
        if k == 0:
            p.wait("tensor", ones_tok)
        last = None
        for ts in range(16):
            last = t and t.matmul(ps_cols[:, 16 * ts + k:16 * ts + k + 1],
                                  xt[b][:, 128 * ts:128 * (ts + 1)],
                                  ones_col, start=True, stop=True,
                                  skip_group_check=True)
            last = t and t.matmul(ps_cols[:, 256 + 16 * ts + k:256 + 16 * ts + k + 1],
                                  sq[b][:, 128 * ts:128 * (ts + 1)],
                                  ones_col, start=True, stop=True,
                                  skip_group_check=True)
        if k == 15:
            tmm = p.dinc("tensor", "td")
        else:
            tmm = p.inc(last, "td", 1)
        xt_free[b] = ("td", tmm)
        sq_free[b] = ("td", tmm)

    if upto == "stats":
        p.wait("sync", ("td", tmm))
        return
    v = p.e("vector")
    p.wait("vector", ("td", tmm))
    _ = v and v.reduce_sum(sred[:, 0:16], ps_cols[:, 0:256].rearrange(
        "p (a b) -> p a b", b=16), axis=AX.X)
    _ = v and v.reduce_sum(sred[:, 16:32], ps_cols[:, 256:512].rearrange(
        "p (a b) -> p a b", b=16), axis=AX.X)
    p.fence("vector", "vd")
    _ = v and v.tensor_single_scalar(mean16, sred[:, 0:16], 1.0 / D, op=ALU.mult)
    p.fence("vector", "vd")
    _ = v and v.tensor_tensor(sv16, mean16, mean16, op=ALU.mult)
    p.fence("vector", "vd")
    _ = v and v.tensor_single_scalar(sv16, sv16, 1e-5, op=ALU.subtract)
    p.fence("vector", "vd")
    _ = v and v.scalar_tensor_tensor(var16, sred[:, 16:32], 1.0 / D, sv16,
                                     op0=ALU.mult, op1=ALU.subtract)
    tv = p.dinc("vector", "vd")
    p.wait("scalar", ("vd", tv))
    s = p.e("scalar")
    _ = s and s.activation(sv16, var16, AF.Sqrt)
    tsv = p.dinc("scalar", "sd")
    p.wait("vector", ("sd", tsv))
    p.fence("vector", "vd")
    _ = v and v.reciprocal(rs16, sv16)
    trs = p.dinc("vector", "vd")
    if dbg_dram is not None:
        dscr = C.at(lay["xt"], 128, 512, F32)
        _ = v and v.tensor_copy(dscr, psum[6][:, 0:512])
        tdbg = p.dinc("vector", "vd")
        g = p.e("gpsimd")
        p.wait("gpsimd", ("vd", tdbg))
        p.inc(g and g.dma_start(out=dbg_dram[:, 0:512], in_=dscr), tag + "db", 16)
        p.inc(g and g.dma_start(out=dbg_dram[:, 512:544], in_=sred), tag + "db", 16)
        p.inc(g and g.dma_start(out=dbg_dram[:, 544:608], in_=s16), tag + "db", 16)
        p.wait("sync", p.tok(tag + "db"))
    if upto == "rows":
        p.wait("sync", ("vd", trs))
        return
    p.wait("scalar", ("vd", trs))
    sqe = p.e("scalar")
    tg = 0
    rsem = tag + "row"
    for ts in range(16):
        tg = p.inc(sqe and sqe.dma_start(out=rows_dram[0:1, 128 * ts:128 * (ts + 1)],
                                         in_=mean16[:, ts:ts + 1]), rsem, 16)
        tg = p.inc(sqe and sqe.dma_start(out=rows_dram[1:2, 128 * ts:128 * (ts + 1)],
                                         in_=rs16[:, ts:ts + 1]), rsem, 16)
    if upto == "gather":
        p.wait("sync", (rsem, tg))
        return
    g = p.e("gpsimd")
    p.wait("gpsimd", (rsem, tg))
    bsem = tag + "bb"
    mb_src = bass.AP(rows_dram.tensor, 0, [[0, 128], [1, 2048]])
    rb_src = bass.AP(rows_dram.tensor, 2048, [[0, 128], [1, 2048]])
    p.inc(g and g.dma_start(out=mB, in_=mb_src), bsem, 16)
    tbb = p.inc(g and g.dma_start(out=rB, in_=rb_src), bsem, 16)
    last_b = (bsem, tbb)

    if upto == "bcast":
        p.wait("sync", last_b)
        return
    xt_free = [last_b, last_b]
    for k in range(16):
        b = k % 2
        g = p.e("gpsimd")
        p.wait("gpsimd", xt_free[b])
        tload = p.inc(g and g.dma_start(out=xt[b], in_=x_dram[128 * k:128 * (k + 1), :]),
                      xsem[b], 16)
        done = out_cb(k, xt[b], mB, rB, (xsem[b], tload))
        xt_free[b] = done



def emit_fc(p, C, psum, *, wt_dram, K, NTILES, rhs_fn, rhs_tok, Tn, tw, evac,
            lay, fcs, wq="sync", cvt="scalar", split_k=1024, tile_off=0):
    km = K // 128
    tcs = Tn // tw
    assert tcs <= 4
    nsp = (K + split_k - 1) // split_k
    NSLOT = 4
    sfp = [C.at(lay["sf"] + s * split_k * 4, 128, split_k, F32) for s in range(NSLOT)]
    sbp = [C.at(lay["sb"] + s * split_k * 2, 128, split_k, BF16) for s in range(NSLOT)]
    if not fcs:
        fcs.update(dma_free=[None] * NSLOT, cvt_free=[None] * NSLOT,
                   cvt_tok=[None] * NSLOT, ps_free=[None, None], gidx=0)
    csem = "sd" if cvt == "scalar" else "vd"
    nfull = wt_dram.shape[1]

    p.wait("tensor", rhs_tok)
    for j in range(NTILES):
        b = j % 2
        jj = j + tile_off
        pieces = []
        for sp in range(nsp):
            s_ = fcs["gidx"] % NSLOT
            fcs["gidx"] += 1
            k0 = sp * split_k
            kk = min(split_k, K - k0)
            w = p.e(wq)
            p.wait(wq, fcs["dma_free"][s_])
            src_ap = bass.AP(wt_dram.tensor, k0 * nfull + 128 * jj,
                             [[nfull, 128], [128 * nfull, kk // 128], [1, 128]])
            wsem = "fw%d" % s_
            twk = p.inc(w and w.dma_start(out=sfp[s_][:, 0:kk], in_=src_ap), wsem, 16)
            cv = p.e(cvt)
            p.wait(cvt, (wsem, twk))
            p.wait(cvt, fcs["cvt_free"][s_])
            if cvt == "scalar":
                _ = cv and cv.copy(sbp[s_][:, 0:kk], sfp[s_][:, 0:kk])
            else:
                _ = cv and cv.tensor_copy(sbp[s_][:, 0:kk], sfp[s_][:, 0:kk])
            tck = p.dinc(cvt, csem)
            fcs["cvt_tok"][s_] = (csem, tck)
            fcs["dma_free"][s_] = (csem, tck)
            pieces.append(s_)
        t = p.e("tensor")
        p.wait("tensor", fcs["ps_free"][b])
        last = None
        seen = set()
        for kc in range(km):
            sp = (kc * 128) // split_k
            s_ = pieces[sp]
            if sp not in seen:
                seen.add(sp)
                p.wait("tensor", fcs["cvt_tok"][s_])
            kin = kc * 128 - sp * split_k
            lhs = sbp[s_][:, kin:kin + 128]
            for tc in range(tcs):
                last = t and t.matmul(psum[tcs * b + tc][:, 0:tw], lhs,
                                      rhs_fn(kc, tc), start=(kc == 0), stop=(kc == km - 1))
        tdk = p.inc(last, "td", 1)
        for s_ in pieces:
            fcs["cvt_free"][s_] = ("td", tdk)
        ps_list = [psum[tcs * b + tc][:, 0:tw] for tc in range(tcs)]
        fcs["ps_free"][b] = evac(j, ps_list, ("td", tdk))



def make_launch2(nc, xi, xo, pref='', upto='full', dbg=False):
    w1t = nc.dram_tensor(pref + "w1t", [2048, 4096], F32, kind="ExternalInput").ap()
    b1c = nc.dram_tensor(pref + "b1c", [128, 32], F32, kind="ExternalInput").ap()
    w2t = nc.dram_tensor(pref + "w2t", [4096, 4096], F32, kind="ExternalInput").ap()
    b2c = nc.dram_tensor(pref + "b2c", [128, 32], F32, kind="ExternalInput").ap()
    w3t = nc.dram_tensor(pref + "w3t", [4096, 2048], F32, kind="ExternalInput").ap()
    b3c = nc.dram_tensor(pref + "b3c", [128, 16], F32, kind="ExternalInput").ap()
    xln_st = nc.dram_tensor(pref + "xln_st", [D, T], BF16,
                            kind="ExternalOutput" if dbg else "Internal").ap()
    rows_d = nc.dram_tensor(pref + "rows_d", [2, 2048], F32,
                            kind="ExternalOutput" if dbg else "Internal").ap()
    dbg_ln = nc.dram_tensor(pref + "dbg_ln", [128, 1024], F32,
                            kind="ExternalOutput").ap() if dbg else None

    KB = 1024

    def emit(p, C, psum):
        lay_ln = dict(xt=24576, sq=40960, mB=57344, rB=65536, rows=73728,
                      s16=90112, ones=90368)
        lay_fc = dict(sf=0, sb=16384)
        xh = C.at(24576, 128, 16 * KB, BF16)
        h1h = C.at(57344, 128, 32 * KB, BF16)
        h2h = C.at(122880, 128, 32 * KB, BF16)
        MISC = 188416
        b1s = C.at(MISC, 128, 32, F32)
        b2s = C.at(MISC + 128, 128, 32, F32)
        b3s = C.at(MISC + 256, 128, 16, F32)
        xit = [C.at(MISC + 512 + k * 4096, 128, KB, F32) for k in range(2)]
        ost = [C.at(MISC + 8704 + k * 4096, 128, KB, F32) for k in range(2)]
        xlns = [C.at(90624 + k * 4096, 128, 2048, BF16) for k in range(2)]

        g = p.e("gpsimd")
        p.inc(g and g.dma_start(out=b1s, in_=b1c), "bia", 16)
        p.inc(g and g.dma_start(out=b2s, in_=b2c), "bia", 16)
        tb = p.inc(g and g.dma_start(out=b3s, in_=b3c), "bia", 16)
        bias_tok = ("bia", tb)

        stage_free = [None, None]

        def ln_out(k, xt_ap, mB, rB, tok):
            b = k % 2
            v = p.e("vector")
            p.wait("vector", tok)
            p.wait("vector", stage_free[b])
            _ = v and v.scalar_tensor_tensor(xt_ap, xt_ap, 1.0, mB, op0=ALU.mult,
                                             op1=ALU.subtract)
            _ = v and v.tensor_tensor(xlns[b], xt_ap, rB, op=ALU.mult)
            tvv = p.dinc("vector", "vd")
            g2 = p.e("gpsimd")
            p.wait("gpsimd", ("vd", tvv))
            tst = p.inc(g2 and g2.dma_start(out=xln_st[128 * k:128 * (k + 1), :],
                                            in_=xlns[b]), "xs%d" % b, 16)
            stage_free[b] = ("xs%d" % b, tst)
            return ("vd", tvv)

        emit_ln(p, C, psum, x_dram=xi, rows_dram=rows_d, out_cb=ln_out, lay=lay_ln, tag="ln", dbg_dram=dbg_ln,
                upto=upto if upto.startswith(("stats", "rows", "gather", "bcast")) else "full")
        if upto.startswith(("stats", "rows", "gather", "bcast")):
            return
        ln_done = [p.tok("xs0"), p.tok("xs1")]
        if upto == "ln":
            p.wait("sync", ln_done)
            return

        fcs = {}
        cross = {}
        for hf in range(2):
            c0 = KB * hf
            g = p.e("gpsimd")
            p.wait("gpsimd", ln_done if hf == 0 else cross["h1_mm"])
            xsem = "xh%d" % hf
            txh = 0
            for kc in range(16):
                txh = p.inc(g and g.dma_start(
                    out=xh[:, KB * kc:KB * (kc + 1)],
                    in_=xln_st[128 * kc:128 * (kc + 1), c0:c0 + KB]), xsem, 16)
            xh_tok = (xsem, txh)

            def rhs1(kc, tc):
                return xh[:, KB * kc + 512 * tc: KB * kc + 512 * (tc + 1)]

            def evac1(j, ps_list, tdtok):
                s = p.e("scalar")
                p.wait("scalar", tdtok)
                if j == 0:
                    p.wait("scalar", bias_tok)
                    p.wait("scalar", cross.get("h2_mm"))
                for tc, ps in enumerate(ps_list):
                    _ = s and s.activation(
                        h1h[:, KB * j + 512 * tc: KB * j + 512 * (tc + 1)], ps,
                        AF.Relu, bias=b1s[:, j:j + 1])
                return ("sd", p.dinc("scalar", "sd"))

            if hf == 0 and not fcs:
                fcs.update(dma_free=[ln_done] * 4, cvt_free=[ln_done] * 4,
                           cvt_tok=[None] * 4, ps_free=[None, None], gidx=0)
            emit_fc(p, C, psum, wt_dram=w1t, K=2048, NTILES=32, rhs_fn=rhs1,
                    rhs_tok=xh_tok, Tn=KB, tw=512, evac=evac1, lay=lay_fc, fcs=fcs)
            h1_tok = p.tok("sd")
            if upto == "h1":
                p.wait("sync", h1_tok)
                p.wait("sync", ln_done)
                return
            cross["h1_mm"] = p.tok("td")

            def rhs2(kc, tc):
                return h1h[:, KB * kc + 512 * tc: KB * kc + 512 * (tc + 1)]

            def evac2(j, ps_list, tdtok):
                s = p.e("scalar")
                p.wait("scalar", tdtok)
                if j == 0:
                    p.wait("scalar", cross.get("w3_mm"))
                for tc, ps in enumerate(ps_list):
                    _ = s and s.activation(
                        h2h[:, KB * j + 512 * tc: KB * j + 512 * (tc + 1)], ps,
                        AF.Relu, bias=b2s[:, j:j + 1])
                return ("sd", p.dinc("scalar", "sd"))

            emit_fc(p, C, psum, wt_dram=w2t, K=4096, NTILES=32, rhs_fn=rhs2,
                    rhs_tok=h1_tok, Tn=KB, tw=512, evac=evac2, lay=lay_fc, fcs=fcs)
            h2_tok = p.tok("sd")
            if upto == "h2":
                p.wait("sync", h2_tok)
                p.wait("sync", ln_done)
                return
            cross["h2_mm"] = p.tok("td")

            xi_free = [None, None]
            ost_free = [None, None]

            def rhs3(kc, tc):
                return h2h[:, KB * kc + 512 * tc: KB * kc + 512 * (tc + 1)]

            def evac3(j, ps_list, tdtok):
                b = j % 2
                g2 = p.e("gpsimd")
                p.wait("gpsimd", xi_free[b])
                txi = p.inc(g2 and g2.dma_start(
                    out=xit[b], in_=xi[128 * j:128 * (j + 1), c0:c0 + KB]),
                    "xi%d" % b, 16)
                s = p.e("scalar")
                p.wait("scalar", tdtok)
                p.wait("scalar", ost_free[b])
                for tc, ps in enumerate(ps_list):
                    _ = s and s.activation(
                        ost[b][:, 512 * tc:512 * (tc + 1)], ps, AF.Identity,
                        bias=b3s[:, j:j + 1])
                tsd = p.dinc("scalar", "sd")
                v = p.e("vector")
                p.wait("vector", ("sd", tsd))
                p.wait("vector", ("xi%d" % b, txi))
                _ = v and v.tensor_add(ost[b], ost[b], xit[b])
                tvv = p.dinc("vector", "vd")
                xi_free[b] = ("vd", tvv)
                g3 = p.e("gpsimd")
                p.wait("gpsimd", ("vd", tvv))
                tst = p.inc(g3 and g3.dma_start(
                    out=xo[128 * j:128 * (j + 1), c0:c0 + KB], in_=ost[b]),
                    "os%d" % b, 16)
                ost_free[b] = ("os%d" % b, tst)
                return ("sd", tsd)

            emit_fc(p, C, psum, wt_dram=w3t, K=4096, NTILES=16, rhs_fn=rhs3,
                    rhs_tok=h2_tok, Tn=KB, tw=512, evac=evac3, lay=lay_fc, fcs=fcs)
            cross["w3_mm"] = p.tok("td")

        p.wait("sync", [p.tok("os0"), p.tok("os1")])

    return emit


def build_launch2(upto='full', dbg=False):
    nc = bass.Bass("TRN2", target_bir_lowering=False)
    xi = nc.dram_tensor("xi", [D, T], F32, kind="ExternalInput").ap()
    xo = nc.dram_tensor("xo", [D, T], F32, kind="ExternalOutput").ap()
    return build_engines(nc, make_launch2(nc, xi, xo, "", upto, dbg))


def run_launch2(d, xs, upto='full', dbg=False):
    nc = build_launch2(upto, dbg)
    in_maps = [dict(xi=xs[c], w1t=d["w1t"], b1c=d["b1c"], w2t=d["w2t"], b2c=d["b2c"],
                    w3t=d["w3t"], b3c=d["b3c"]) for c in range(len(xs))]
    res = run_bass_kernel_spmd(nc, in_maps, core_ids=list(range(len(xs))))
    return res.results




def emit_attention(p, C, psum, *, nheads, sc, q_dram, k_dram, v_dram,
                   nrow_d, lrow_d, o_sb, lay, masks, tag,
                   qr_dram=None, krot_sb=None, head_free_tok=None):
    mdiag, m1s = masks
    qp = [C.at(lay["qp"] + i * 4096, 128, 2048, BF16) for i in range(2)]
    kp = [C.at(lay["kp"] + i * 4096, 128, 2048, BF16) for i in range(2)]
    vv_ = [C.at(lay["vv"] + i * 4096, 128, 2048, BF16) for i in range(2)]
    qr = [C.at(lay["qr"] + i * 4096, 64, 2048, BF16) for i in range(2)] \
        if qr_dram is not None else None
    linvB = [C.at(lay["linvB"] + i * 8192, 128, 2048, F32) for i in range(2)]
    negm = [C.at(lay["negm"] + i * 4096, 1, 2048, BF16) for i in range(2)]
    scr = [C.at(lay["scr"] + i * 2048, 128, 512, F32) for i in range(2)]
    pT = [C.at(lay["pT"] + i * 1024, 128, 512, BF16) for i in range(2)]
    dg = C.at(lay["dg"], 128, 128, F32)
    ST = [C.at(lay["ST"] + i * 256, 128, 64, F32) for i in range(2)]
    NM = [C.at(lay["NM"] + i * 32, 128, 16, BF16) for i in range(2)]
    LV = [C.at(lay["LV"] + i * 64, 128, 16, F32) for i in range(2)]
    ones_bf = C.at(lay["ones_bf"], 1, 128, BF16)

    v = p.e("vector")
    ones_tok = ("vd", p.dinc("vector", "vd")) if (v and v.memset(ones_bf, 1.0)) or True \
        else None

    slot_free = [head_free_tok, head_free_tok]
    ps_tok = [None, None]
    for h in range(nheads):
        hb = h % 2
        g = p.e("gpsimd")
        p.wait("gpsimd", slot_free[hb])
        tq = p.inc(g and g.dma_start(out=qp[hb],
                                     in_=q_dram[128 * h:128 * (h + 1), :]),
                   tag + "q%d" % hb, 16)
        tk = p.inc(g and g.dma_start(out=kp[hb],
                                     in_=k_dram[128 * h:128 * (h + 1), :]),
                   tag + "k%d" % hb, 16)
        toks = [(tag + "q%d" % hb, tq), (tag + "k%d" % hb, tk)]
        if qr is not None:
            tr = p.inc(g and g.dma_start(out=qr[hb],
                                         in_=qr_dram[64 * h:64 * (h + 1), :]),
                       tag + "r%d" % hb, 16)
            toks.append((tag + "r%d" % hb, tr))
        p.wait("tensor", toks)
        if h == 0:
            p.wait("tensor", ones_tok)
        st_ = ST[hb]
        mrow_c = st_[:, 0:8]; biasc = st_[:, 8:16]
        zra = st_[:, 16:24]; zrb = st_[:, 24:32]
        m_c = st_[:, 32:33]; mp_c = st_[:, 33:34]; z_c = st_[:, 34:35]
        dd = st_[:, 40:48]; fsb = st_[:, 48:56]; zw = st_[:, 56:64]
        for qt in range(16):
            vv2 = p.e("vector")
            p.wait("vector", ps_tok[0])
            p.wait("vector", ps_tok[1])
            _ = vv2 and vv2.memset(zra, 0.0)
            _ = vv2 and vv2.memset(zrb, 0.0)
            tz = p.dinc("vector", "vd")
            nkc = (128 * (qt + 1) + 511) // 512
            for kc in range(nkc):
                w = min(512, 128 * (qt + 1) - 512 * kc)
                bank = psum[6 + (kc % 2)]
                t = p.e("tensor")
                p.wait("tensor", ps_tok[kc % 2])
                last = t and t.matmul(bank[:, 0:w], qp[hb][:, 128 * qt:128 * (qt + 1)],
                                      kp[hb][:, 512 * kc:512 * kc + w],
                                      start=True, stop=(qr is None))
                if qr is not None:
                    last = t and t.matmul(bank[:, 0:w], qr[hb][:, 128 * qt:128 * (qt + 1)],
                                          krot_sb[:, 512 * kc:512 * kc + w],
                                          start=False, stop=True)
                tdk = p.inc(last, "td", 1)
                vv2 = p.e("vector")
                p.wait("vector", ("td", tdk))
                diag = (kc == nkc - 1)
                if diag:
                    _ = vv2 and vv2.scalar_tensor_tensor(dg, bank[:, w - 128:w], 0.0,
                                                         mdiag, op0=ALU.add, op1=ALU.add)
                _ = vv2 and vv2.reduce_max(mrow_c[:, kc:kc + 1], bank[:, 0:w], axis=AX.X)
                p.fence("vector", "vd")
                _ = vv2 and vv2.tensor_scalar(biasc[:, kc:kc + 1], mrow_c[:, kc:kc + 1],
                                              -sc, None, op0=ALU.mult)
                tvb = p.dinc("vector", "vd")
                s = p.e("scalar")
                p.wait("scalar", ("vd", tvb))
                if qt == 0 and kc == 0:
                    p.wait("scalar", ("vd", tz))
                nd_w = (w - 128) if diag else w
                if nd_w > 0:
                    _ = s and s.activation(scr[kc % 2][:, 0:nd_w], bank[:, 0:nd_w],
                                           AF.Exp, bias=biasc[:, kc:kc + 1], scale=sc,
                                           accum_out=zra[:, kc:kc + 1])
                if diag:
                    _ = s and s.activation(scr[kc % 2][:, 384:512], dg, AF.Exp,
                                           bias=biasc[:, kc:kc + 1], scale=sc,
                                           accum_out=zrb[:, kc:kc + 1])
                tse = p.dinc("scalar", "sd")
                ps_tok[kc % 2] = ("sd", tse)
            vv2 = p.e("vector")
            p.wait("vector", ps_tok[(nkc - 1) % 2])
            _ = vv2 and vv2.reduce_max(m_c, mrow_c[:, 0:nkc], axis=AX.X)
            p.fence("vector", "vd")
            _ = vv2 and vv2.tensor_scalar(NM[hb][:, qt:qt + 1], m_c, -1.0, None,
                                          op0=ALU.mult)
            p.fence("vector", "vd")
            _ = vv2 and vv2.tensor_scalar(mp_c, NM[hb][:, qt:qt + 1], -1.0, None,
                                          op0=ALU.mult)
            p.fence("vector", "vd")
            _ = vv2 and vv2.tensor_scalar(dd[:, 0:nkc], mrow_c[:, 0:nkc], mp_c, sc,
                                          op0=ALU.subtract, op1=ALU.mult)
            tvd = p.dinc("vector", "vd")
            s = p.e("scalar")
            p.wait("scalar", ("vd", tvd))
            _ = s and s.activation(fsb[:, 0:nkc], dd[:, 0:nkc], AF.Exp)
            tsf = p.dinc("scalar", "sd")
            vv2 = p.e("vector")
            p.wait("vector", ("sd", tsf))
            _ = vv2 and vv2.tensor_add(zw[:, 0:nkc], zra[:, 0:nkc], zrb[:, 0:nkc])
            p.fence("vector", "vd")
            _ = vv2 and vv2.tensor_mul(zw[:, 0:nkc], zw[:, 0:nkc], fsb[:, 0:nkc])
            p.fence("vector", "vd")
            _ = vv2 and vv2.reduce_sum(z_c, zw[:, 0:nkc], axis=AX.X)
            p.fence("vector", "vd")
            _ = vv2 and vv2.reciprocal(LV[hb][:, qt:qt + 1], z_c)
            p.fence("vector", "vd")
        thd = p.dinc("vector", "vd")
        sq = p.e("scalar")
        p.wait("scalar", ("vd", thd))
        gsem = tag + "g%d" % hb
        tg = 0
        for qt in range(16):
            tg = p.inc(sq and sq.dma_start(out=nrow_d[h:h + 1, 128 * qt:128 * (qt + 1)],
                                           in_=NM[hb][:, qt:qt + 1]), gsem, 16)
            tg = p.inc(sq and sq.dma_start(out=lrow_d[h:h + 1, 128 * qt:128 * (qt + 1)],
                                           in_=LV[hb][:, qt:qt + 1]), gsem, 16)
        slot_free[hb] = (gsem, tg)
    gather_done = [p.tok(tag + "g0"), p.tok(tag + "g1")]

    slot_free = [gather_done, gather_done]
    obank_free = [None] * 4
    pT_free = [None, None]
    psT_free = [None, None]
    for h in range(nheads):
        hb = h % 2
        g = p.e("gpsimd")
        p.wait("gpsimd", slot_free[hb])
        tq = p.inc(g and g.dma_start(out=qp[hb],
                                     in_=q_dram[128 * h:128 * (h + 1), :]),
                   tag + "q%d" % hb, 16)
        tk = p.inc(g and g.dma_start(out=kp[hb],
                                     in_=k_dram[128 * h:128 * (h + 1), :]),
                   tag + "k%d" % hb, 16)
        tv_ = 0
        for kt in range(16):
            tv_ = p.inc(g and g.dma_start(
                out=vv_[hb][:, 128 * kt:128 * (kt + 1)],
                in_=v_dram[128 * kt:128 * (kt + 1), 128 * h:128 * (h + 1)]),
                tag + "v%d" % hb, 16)
        toks = [(tag + "q%d" % hb, tq), (tag + "k%d" % hb, tk),
                (tag + "v%d" % hb, tv_)]
        if qr is not None:
            tr = p.inc(g and g.dma_start(out=qr[hb],
                                         in_=qr_dram[64 * h:64 * (h + 1), :]),
                       tag + "r%d" % hb, 16)
            toks.append((tag + "r%d" % hb, tr))
        sq = p.e("scalar")
        if h < 2:
            p.wait("scalar", gather_done)
        p.wait("scalar", slot_free[hb])
        tn = p.inc(sq and sq.dma_start(out=negm[hb], in_=nrow_d[h:h + 1, :]),
                   tag + "n%d" % hb, 16)
        lb_src = bass.AP(lrow_d.tensor, 2048 * h, [[0, 128], [1, 2048]])
        tl = p.inc(sq and sq.dma_start(out=linvB[hb], in_=lb_src),
                   tag + "l%d" % hb, 16)
        toks.append((tag + "n%d" % hb, tn))
        p.wait("tensor", toks)
        lv_tok = (tag + "l%d" % hb, tl)
        pp = 0
        last_mm_tok = None
        for kt in range(16):
            for qc in range(kt // 4, 4):
                bank = psum[4 + pp]
                t = p.e("tensor")
                p.wait("tensor", psT_free[pp])
                last = t and t.matmul(bank[:, 0:512], kp[hb][:, 128 * kt:128 * (kt + 1)],
                                      qp[hb][:, 512 * qc:512 * (qc + 1)],
                                      start=True, stop=False)
                if qr is not None:
                    last = t and t.matmul(bank[:, 0:512], krot_sb[:, 128 * kt:128 * (kt + 1)],
                                          qr[hb][:, 512 * qc:512 * (qc + 1)],
                                          start=False, stop=False)
                last = t and t.matmul(bank[:, 0:512], ones_bf,
                                      negm[hb][0:1, 512 * qc:512 * (qc + 1)],
                                      start=False, stop=True)
                tdk = p.inc(last, "td", 1)
                s = p.e("scalar")
                p.wait("scalar", ("td", tdk))
                p.wait("scalar", pT_free[pp])
                _ = s and s.activation(pT[pp], bank[:, 0:512], AF.Exp, scale=sc)
                tpe = p.dinc("scalar", "sd")
                psT_free[pp] = ("sd", tpe)
                pt_tok = ("sd", tpe)
                if qc == kt // 4:
                    vv2 = p.e("vector")
                    p.wait("vector", ("sd", tpe))
                    _ = vv2 and vv2.tensor_mul(pT[pp], pT[pp], m1s[kt % 4])
                    tpm = p.dinc("vector", "vd")
                    pt_tok = ("vd", tpm)
                t = p.e("tensor")
                p.wait("tensor", pt_tok)
                if kt == 0:
                    p.wait("tensor", obank_free[qc])
                lmm = t and t.matmul(psum[qc][:, 0:512],
                                     vv_[hb][:, 128 * kt:128 * (kt + 1)], pT[pp],
                                     start=(kt == 0), stop=(kt == 4 * qc + 3))
                tdo = p.inc(lmm, "td", 1)
                pT_free[pp] = ("td", tdo)
                last_mm_tok = ("td", tdo)
                if kt == 4 * qc + 3:
                    vv2 = p.e("vector")
                    p.wait("vector", ("td", tdo))
                    p.wait("vector", lv_tok)
                    _ = vv2 and vv2.tensor_tensor(
                        o_sb[:, 2048 * h + 512 * qc:2048 * h + 512 * (qc + 1)],
                        psum[qc][:, 0:512], linvB[hb][:, 512 * qc:512 * (qc + 1)],
                        op=ALU.mult)
                    teo = p.dinc("vector", "vd")
                    obank_free[qc] = ("vd", teo)
                pp ^= 1
        slot_free[hb] = [last_mm_tok, obank_free[3]]
    return [p.tok("vd"), p.tok("td")]




def make_launch3(nc, xi, xo, pref='', dbg=False):
    wq1 = nc.dram_tensor(pref + "wq1", [2048, 2048], F32, kind="ExternalInput").ap()
    wk1 = nc.dram_tensor(pref + "wk1", [2048, 2048], F32, kind="ExternalInput").ap()
    wv1 = nc.dram_tensor(pref + "wv1", [2048, 2048], F32, kind="ExternalInput").ap()
    wo1 = nc.dram_tensor(pref + "wo1", [2048, 2048], F32, kind="ExternalInput").ap()
    mdiag_d = nc.dram_tensor(pref + "mdiag", [128, 128], F32, kind="ExternalInput").ap()
    m1s_d = nc.dram_tensor(pref + "m1s", [4, 128, 512], BF16, kind="ExternalInput").ap()
    qs = nc.dram_tensor(pref + "qs", [2048, T], BF16, kind="Internal").ap()
    ks = nc.dram_tensor(pref + "ks", [2048, T], BF16, kind="Internal").ap()
    vs = nc.dram_tensor(pref + "vs", [T, 2048], BF16, kind="Internal").ap()
    nrow = nc.dram_tensor(pref + "nrow", [16, T], BF16, kind="Internal").ap()
    lrow = nc.dram_tensor(pref + "lrow", [16, T], F32, kind="Internal").ap()
    rows_d = nc.dram_tensor(pref + "rows_d", [2, 2048], F32, kind="Internal").ap()

    def emit(p, C, psum):
        lay_ln = dict(xt=90112, sq=106496, mB=122880, rB=131072, s16=139264,
                      ones=139776)
        lay_fc = dict(sf=0, sb=16384)
        lay_at = dict(qp=0, kp=8192, vv=16384, linvB=155648, scr=172032,
                      pT=176128, dg=178176, ST=178688, NM=179200, LV=179264,
                      ones_bf=179392, negm=179648, qr=0)
        xln = C.at(24576, 128, 16 * 2048, BF16)
        o_sb = C.at(90112, 128, 16 * 2048, BF16)
        wvt = [C.at(90112 + dc * 4096, 128, 2048, BF16) for dc in range(16)]
        wvf = C.at(0, 128, 2048, F32)
        mdiag = C.at(187840, 128, 128, F32)
        m1s = [C.at(188352 + i * 1024, 128, 512, BF16) for i in range(4)]
        stg = [C.at(192448 + i * 4096, 128, 2048, BF16) for i in range(2)]
        xit = [C.at(155648 + i * 8192, 128, 2048, F32) for i in range(2)]
        ost = [C.at(172032 + i * 8192, 128, 2048, F32) for i in range(2)]

        g = p.e("gpsimd")
        p.inc(g and g.dma_start(out=mdiag, in_=mdiag_d), "msk", 16)
        for i in range(4):
            tm = p.inc(g and g.dma_start(out=m1s[i], in_=m1s_d[i]), "msk", 16)
        mask_tok = ("msk", tm)

        def ln_out(k, xt_ap, mB, rB, tok):
            v = p.e("vector")
            p.wait("vector", tok)
            _ = v and v.scalar_tensor_tensor(xt_ap, xt_ap, 1.0, mB, op0=ALU.mult,
                                             op1=ALU.subtract)
            _ = v and v.tensor_tensor(xln[:, 2048 * k:2048 * (k + 1)], xt_ap, rB,
                                      op=ALU.mult)
            return ("vd", p.dinc("vector", "vd"))

        emit_ln(p, C, psum, x_dram=xi, rows_dram=rows_d, out_cb=ln_out,
                lay=lay_ln, tag="ln")
        ln_done = p.tok("vd")

        fcs = {}
        stg_free = [None, None]

        def store_fc(wt, dst, ssem):
            def rhs(kc, tc):
                return xln[:, 2048 * kc + 512 * tc:2048 * kc + 512 * (tc + 1)]

            def evac(j, ps_list, tdtok):
                b = j % 2
                v = p.e("vector")
                p.wait("vector", tdtok)
                p.wait("vector", stg_free[b])
                for tc, ps in enumerate(ps_list):
                    _ = v and v.tensor_copy(stg[b][:, 512 * tc:512 * (tc + 1)], ps)
                tvv = p.dinc("vector", "vd")
                g2 = p.e("gpsimd")
                p.wait("gpsimd", ("vd", tvv))
                tst = p.inc(g2 and g2.dma_start(
                    out=dst[128 * j:128 * (j + 1), :], in_=stg[b]),
                    ssem + "%d" % b, 16)
                stg_free[b] = (ssem + "%d" % b, tst)
                return ("vd", tvv)

            emit_fc(p, C, psum, wt_dram=wt, K=2048, NTILES=16, rhs_fn=rhs,
                    rhs_tok=ln_done, Tn=2048, tw=512, evac=evac, lay=lay_fc,
                    fcs=fcs)

        store_fc(wq1, qs, "qS")
        store_fc(wk1, ks, "kS")
        qk_done = [p.tok("qS0"), p.tok("qS1"), p.tok("kS0"), p.tok("kS1")]

        qkmm_done = p.tok("td")
        last_cvt = None
        for dc in range(16):
            s_ = p.e("sync")
            if dc == 0:
                p.wait("sync", ln_done)
                p.wait("sync", qkmm_done)
            else:
                p.wait("sync", ("sd", last_cvt))
            tw_ = p.inc(s_ and s_.dma_start(out=wvf, in_=wv1[128 * dc:128 * (dc + 1), :]),
                        "wvl", 16)
            sc_ = p.e("scalar")
            p.wait("scalar", ("wvl", tw_))
            _ = sc_ and sc_.copy(wvt[dc], wvf)
            last_cvt = p.dinc("scalar", "sd")
        wv_tok = ("sd", last_cvt)

        vps_free = [None, None]
        for tt in range(16):
            t = p.e("tensor")
            p.wait("tensor", wv_tok)
            p.wait("tensor", vps_free[tt % 2])
            last = None
            for dc in range(16):
                for vc in range(4):
                    last = t and t.matmul(psum[4 * (tt % 2) + vc][:, 0:512],
                                          xln[:, 2048 * dc + 128 * tt:2048 * dc + 128 * (tt + 1)],
                                          wvt[dc][:, 512 * vc:512 * (vc + 1)],
                                          start=(dc == 0), stop=(dc == 15))
            tdk = p.inc(last, "td", 1)
            b = tt % 2
            v = p.e("vector")
            p.wait("vector", ("td", tdk))
            p.wait("vector", stg_free[b])
            for vc in range(4):
                _ = v and v.tensor_copy(stg[b][:, 512 * vc:512 * (vc + 1)],
                                        psum[4 * b + vc][:, 0:512])
            tvv = p.dinc("vector", "vd")
            vps_free[b] = ("vd", tvv)
            g2 = p.e("gpsimd")
            p.wait("gpsimd", ("vd", tvv))
            tst = p.inc(g2 and g2.dma_start(out=vs[128 * tt:128 * (tt + 1), :],
                                            in_=stg[b]), "vS%d" % b, 16)
            stg_free[b] = ("vS%d" % b, tst)
        v_done = [p.tok("vS0"), p.tok("vS1")]

        head_free = qk_done + v_done + [mask_tok]
        at_done = emit_attention(p, C, psum, nheads=16, sc=1.0 / float(np.sqrt(128.0)),
                                 q_dram=qs, k_dram=ks, v_dram=vs, nrow_d=nrow,
                                 lrow_d=lrow, o_sb=o_sb, lay=lay_at,
                                 masks=(mdiag, m1s), tag="at",
                                 head_free_tok=head_free)

        fcs["dma_free"] = [at_done] * 4
        fcs["cvt_free"] = [at_done] * 4
        fcs["ps_free"] = [at_done, at_done]
        xi_free = [at_done, at_done]
        ost_free = [at_done, at_done]

        def rhs_o(kc, tc):
            return o_sb[:, 2048 * kc + 512 * tc:2048 * kc + 512 * (tc + 1)]

        def evac_o(j, ps_list, tdtok):
            b = j % 2
            g2 = p.e("gpsimd")
            p.wait("gpsimd", xi_free[b])
            txi = p.inc(g2 and g2.dma_start(out=xit[b],
                                            in_=xi[128 * j:128 * (j + 1), :]),
                        "xi%d" % b, 16)
            v = p.e("vector")
            p.wait("vector", tdtok)
            p.wait("vector", ("xi%d" % b, txi))
            p.wait("vector", ost_free[b])
            for tc, ps in enumerate(ps_list):
                _ = v and v.tensor_add(ost[b][:, 512 * tc:512 * (tc + 1)], ps,
                                       xit[b][:, 512 * tc:512 * (tc + 1)])
            tvv = p.dinc("vector", "vd")
            xi_free[b] = ("vd", tvv)
            g3 = p.e("gpsimd")
            p.wait("gpsimd", ("vd", tvv))
            tst = p.inc(g3 and g3.dma_start(out=xo[128 * j:128 * (j + 1), :],
                                            in_=ost[b]), "oS%d" % b, 16)
            ost_free[b] = ("oS%d" % b, tst)
            return ("vd", tvv)

        emit_fc(p, C, psum, wt_dram=wo1, K=2048, NTILES=16, rhs_fn=rhs_o,
                rhs_tok=at_done, Tn=2048, tw=512, evac=evac_o, lay=lay_fc, fcs=fcs)

        p.wait("sync", [p.tok("oS0"), p.tok("oS1")])

    return emit


def build_launch3(dbg=False):
    nc = bass.Bass("TRN2", target_bir_lowering=False)
    xi = nc.dram_tensor("xi", [D, T], F32, kind="ExternalInput").ap()
    xo = nc.dram_tensor("xo", [D, T], F32, kind="ExternalOutput").ap()
    return build_engines(nc, make_launch3(nc, xi, xo, "", dbg))


def run_launch3(d, xs, dbg=False):
    nc = build_launch3(dbg)
    in_maps = [dict(xi=xs[c], wq1=d["wq1"], wk1=d["wk1"], wv1=d["wv1"],
                    wo1=d["wo1"], mdiag=d["mdiag"], m1s=d["m1s"])
               for c in range(len(xs))]
    res = run_bass_kernel_spmd(nc, in_maps, core_ids=list(range(len(xs))))
    return res.results




def make_launch1(nc, xi, xo, pref='', dbg=False, upto='full'):
    wqa = nc.dram_tensor(pref + "wqa", [2048, 1536], F32, kind="ExternalInput").ap()
    qnw = nc.dram_tensor(pref + "qnw", [128, 12], F32, kind="ExternalInput").ap()
    wqb = nc.dram_tensor(pref + "wqb", [1536, 3072], F32, kind="ExternalInput").ap()
    wkr = nc.dram_tensor(pref + "wkr", [2048, 64], F32, kind="ExternalInput").ap()
    wkva = nc.dram_tensor(pref + "wkva", [2048, 512], F32, kind="ExternalInput").ap()
    kvnw = nc.dram_tensor(pref + "kvnw", [128, 4], F32, kind="ExternalInput").ap()
    wkvb = nc.dram_tensor(pref + "wkvb", [512, 4096], F32, kind="ExternalInput").ap()
    wo0 = nc.dram_tensor(pref + "wo0", [2048, 2048], F32, kind="ExternalInput").ap()
    cos4_d = nc.dram_tensor(pref + "cos4", [128, T], BF16, kind="ExternalInput").ap()
    sin4_d = nc.dram_tensor(pref + "sin4", [128, T], BF16, kind="ExternalInput").ap()
    mdiag_d = nc.dram_tensor(pref + "mdiag", [128, 128], F32, kind="ExternalInput").ap()
    m1s_d = nc.dram_tensor(pref + "m1s", [4, 128, 512], BF16, kind="ExternalInput").ap()
    qs = nc.dram_tensor(pref + "qs", [2048, T], BF16, kind="Internal").ap()
    qrs = nc.dram_tensor(pref + "qrs", [1024, T], BF16, kind="Internal").ap()
    ks = nc.dram_tensor(pref + "ks", [2048, T], BF16, kind="Internal").ap()
    vs = nc.dram_tensor(pref + "vs", [T, 2048], BF16, kind="Internal").ap()
    nrow = nc.dram_tensor(pref + "nrow", [16, T], BF16, kind="Internal").ap()
    lrow = nc.dram_tensor(pref + "lrow", [16, T], F32, kind="Internal").ap()
    rows_d = nc.dram_tensor(pref + "rows_d", [2, 2048], F32, kind="Internal").ap()
    qrow_d = nc.dram_tensor(pref + "qrow_d", [2, 2048], F32, kind="Internal").ap()

    def emit(p, C, psum):
        lay_ln = dict(xt=90112, sq=106496, mB=122880, rB=131072, s16=178688,
                      ones=179072)
        lay_fc = dict(sf=0, sb=16384)
        lay_at = dict(qp=0, kp=8192, vv=16384, qr=24576, negm=32768,
                      linvB=155648, scr=172032, pT=176128, dg=178176, ST=178688,
                      NM=179200, LV=179264, ones_bf=179392)
        xln = C.at(24576, 128, 16 * 2048, BF16)
        o_sb = C.at(90112, 128, 16 * 2048, BF16)
        qa_bf = C.at(90112, 128, 12 * 2048, BF16)
        kn = C.at(139264, 128, 4 * 2048, BF16)
        rstage = [C.at(155648 + i * 4096, 128, 2048, BF16) for i in range(2)]
        sqs = C.at(163840, 128, 2048, F32)
        sqb = C.at(182272, 128, 2048, BF16)
        rsc = C.at(172032, 128, 512, F32)
        rstdB = C.at(174080, 128, 2048, F32)
        stg = [C.at(182272 + i * 4096, 128, 2048, BF16) for i in range(2)]
        xit = [C.at(155648 + i * 8192, 128, 2048, F32) for i in range(2)]
        ost = [C.at(172032 + i * 8192, 128, 2048, F32) for i in range(2)]
        cos4 = C.at(194816, 128, 2048, BF16)
        sin4 = C.at(198912, 128, 2048, BF16)
        m1s = [C.at(203008 + i * 1024, 128, 512, BF16) for i in range(4)]
        mdiag = C.at(207104, 128, 128, F32)
        krot_sb = C.at(207616, 64, 2048, BF16)

        g = p.e("gpsimd")
        p.inc(g and g.dma_start(out=mdiag, in_=mdiag_d), "msk", 16)
        p.inc(g and g.dma_start(out=cos4, in_=cos4_d), "msk", 16)
        p.inc(g and g.dma_start(out=sin4, in_=sin4_d), "msk", 16)
        for i in range(4):
            tm = p.inc(g and g.dma_start(out=m1s[i], in_=m1s_d[i]), "msk", 16)
        mask_tok = ("msk", tm)
        qnw_s = C.at(211712, 128, 12, F32)
        kvnw_s = C.at(211776, 128, 4, F32)
        p.inc(g and g.dma_start(out=qnw_s, in_=qnw), "msk", 16)
        tm2 = p.inc(g and g.dma_start(out=kvnw_s, in_=kvnw), "msk", 16)
        nw_tok = ("msk", tm2)

        def ln_out(k, xt_ap, mB, rB, tok):
            v = p.e("vector")
            p.wait("vector", tok)
            _ = v and v.scalar_tensor_tensor(xt_ap, xt_ap, 1.0, mB, op0=ALU.mult,
                                             op1=ALU.subtract)
            _ = v and v.tensor_tensor(xln[:, 2048 * k:2048 * (k + 1)], xt_ap, rB,
                                      op=ALU.mult)
            return ("vd", p.dinc("vector", "vd"))

        emit_ln(p, C, psum, x_dram=xi, rows_dram=rows_d, out_cb=ln_out,
                lay=lay_ln, tag="ln")
        ln_done = p.tok("vd")
        if upto == "ln":
            p.wait("sync", ln_done)
            return

        fcs = {}

        def lowrank(wt, K_, ntiles, dst_bf, stat_bank, row_gate=None):
            for hf in range(2):
                def rhs(kc, tc, hf=hf):
                    return xln[:, 2048 * kc + 1024 * hf + 512 * tc:
                               2048 * kc + 1024 * hf + 512 * (tc + 1)]

                def evac(j, ps_list, tdtok, hf=hf):
                    s = p.e("scalar")
                    p.wait("scalar", tdtok)
                    for tc, ps in enumerate(ps_list):
                        _ = s and s.activation(sqb[:, 1024 * hf + 512 * tc:
                                                   1024 * hf + 512 * (tc + 1)], ps,
                                               AF.Square)
                    tsq = p.dinc("scalar", "sd")
                    v = p.e("vector")
                    p.wait("vector", tdtok)
                    p.wait("vector", ("sd", tsq))
                    for tc, ps in enumerate(ps_list):
                        _ = v and v.tensor_copy(
                            dst_bf[:, 2048 * j + 1024 * hf + 512 * tc:
                                   2048 * j + 1024 * hf + 512 * (tc + 1)], ps)
                    tcv = p.dinc("vector", "vd")
                    t = p.e("tensor")
                    p.wait("tensor", ("sd", tsq))
                    if j == 0 and hf == 0:
                        p.wait("tensor", row_gate)
                    last = None
                    for ts8 in range(8):
                        ts = 8 * hf + ts8
                        last = t and t.matmul(
                            psum[stat_bank][:, ntiles * ts + j:ntiles * ts + j + 1],
                            sqb[:, 1024 * hf + 128 * ts8:1024 * hf + 128 * (ts8 + 1)],
                            ones_rms, start=True, stop=True, skip_group_check=True)
                    trm = p.inc(last, "td", 1)
                    return [("vd", tcv), ("td", trm)]

                emit_fc(p, C, psum, wt_dram=wt, K=K_, NTILES=ntiles, rhs_fn=rhs,
                        rhs_tok=ln_done, Tn=1024, tw=512, evac=evac, lay=lay_fc,
                        fcs=fcs)

        ones_rms = C.at(211840, 128, 1, BF16)
        v = p.e("vector")
        _ = v and v.memset(ones_rms, 1.0)
        p.dinc("vector", "vd")

        lowrank(wqa, 2048, 12, qa_bf, 4)
        qa_mm = p.tok("td")
        if upto == "qa":
            p.wait("sync", qa_mm)
            return

        rstat = C.at(211968 - 256 if False else 179200, 128, 48, F32)

        def rms_rows(stat_bank, ntiles, nfeat, row_slot, gate):
            vv2 = p.e("vector")
            p.wait("vector", gate)
            r16 = rstat[:, 0:16]
            _ = vv2 and vv2.reduce_sum(
                r16, psum[stat_bank][:, 0:16 * ntiles].rearrange(
                    "p (a b) -> p a b", b=ntiles), axis=AX.X)
            p.fence("vector", "vd")
            _ = vv2 and vv2.tensor_scalar(r16, r16, 1.0 / nfeat, 1e-6,
                                          op0=ALU.mult, op1=ALU.add)
            tv = p.dinc("vector", "vd")
            s = p.e("scalar")
            p.wait("scalar", ("vd", tv))
            _ = s and s.activation(r16, r16, AF.Sqrt)
            tsv = p.dinc("scalar", "sd")
            p.wait("vector", ("sd", tsv))
            _ = vv2 and vv2.reciprocal(r16, r16)
            trc = p.dinc("vector", "vd")
            sq_ = p.e("scalar")
            p.wait("scalar", ("vd", trc))
            gsem = "qg%d" % row_slot
            tg = 0
            for ts in range(16):
                tg = p.inc(sq_ and sq_.dma_start(
                    out=qrow_d[row_slot:row_slot + 1, 128 * ts:128 * (ts + 1)],
                    in_=r16[:, ts:ts + 1]), gsem, 16)
            g2 = p.e("gpsimd")
            p.wait("gpsimd", (gsem, tg))
            bsrc = bass.AP(qrow_d.tensor, 2048 * row_slot, [[0, 128], [1, 2048]])
            tbb = p.inc(g2 and g2.dma_start(out=rstdB, in_=bsrc), "qb%d" % row_slot, 16)
            return ("qb%d" % row_slot, tbb), ("vd", p.cnt["vd"])

        rb_tok, rows_read_tok = rms_rows(4, 12, 1536, 0, qa_mm)
        vv = p.e("vector")
        p.wait("vector", rb_tok)
        p.wait("vector", nw_tok)
        for j in range(12):
            _ = vv and vv.tensor_tensor(qa_bf[:, 2048 * j:2048 * (j + 1)],
                                        qa_bf[:, 2048 * j:2048 * (j + 1)], rstdB,
                                        op=ALU.mult)
            _ = vv and vv.tensor_scalar(qa_bf[:, 2048 * j:2048 * (j + 1)],
                                        qa_bf[:, 2048 * j:2048 * (j + 1)],
                                        qnw_s[:, j:j + 1], None, op0=ALU.mult)
        qn_tok = ("vd", p.dinc("vector", "vd"))
        if upto == "rms":
            p.wait("sync", qn_tok)
            return

        lowrank(wkva, 2048, 4, kn, 5, row_gate=rows_read_tok)
        kva_mm = p.tok("td")
        rb_tok2, _ = rms_rows(5, 4, 512, 1, kva_mm)
        p.wait("vector", rb_tok2)
        for j in range(4):
            _ = vv and vv.tensor_tensor(kn[:, 2048 * j:2048 * (j + 1)],
                                        kn[:, 2048 * j:2048 * (j + 1)], rstdB,
                                        op=ALU.mult)
            _ = vv and vv.tensor_scalar(kn[:, 2048 * j:2048 * (j + 1)],
                                        kn[:, 2048 * j:2048 * (j + 1)],
                                        kvnw_s[:, j:j + 1], None, op0=ALU.mult)
        kn_tok = ("vd", p.dinc("vector", "vd"))

        kr_f = C.at(0, 128, 1024, F32)
        kr_b = C.at(16384, 128, 1024, BF16)
        s_ = p.e("sync")
        p.wait("sync", p.tok("td"))
        src_ap = bass.AP(wkr.tensor, 0, [[64, 128], [64 * 128, 16], [1, 64]])
        tkr = p.inc(s_ and s_.dma_start(out=kr_f, in_=src_ap), "krl", 16)
        sc_ = p.e("scalar")
        p.wait("scalar", ("krl", tkr))
        _ = sc_ and sc_.copy(kr_b, kr_f)
        tkc = p.dinc("scalar", "sd")
        t = p.e("tensor")
        p.wait("tensor", ("sd", tkc))
        p.wait("tensor", p.tok("vd"))
        last = None
        for tc in range(4):
            for kc in range(16):
                last = t and t.matmul(psum[tc][0:64, 0:512], kr_b[:, 64 * kc:64 * (kc + 1)],
                                      xln[:, 2048 * kc + 512 * tc:2048 * kc + 512 * (tc + 1)],
                                      start=(kc == 0), stop=(kc == 15))
        tkm = p.inc(last, "td", 1)
        kr2 = C.at(174080, 32, 512, BF16)
        vv = p.e("vector")
        g2 = p.e("gpsimd")
        p.wait("vector", ("td", tkm))
        p.wait("vector", mask_tok)
        tdm = 0
        for tc in range(4):
            ps_ = psum[tc]
            sl = slice(512 * tc, 512 * (tc + 1))
            if tc > 0:
                p.wait("vector", ("krx", tdm))
            _ = vv and vv.tensor_copy(sqs[0:32, 0:512], ps_[0:32, 0:512])
            _ = vv and vv.tensor_copy(sqs[32:64, 512:1024], ps_[32:64, 0:512])
            tkv = p.dinc("vector", "vd")
            p.wait("gpsimd", ("vd", tkv))
            tdm = p.inc(g2 and g2.dma_start(out=sqs[0:32, 512:1024],
                                            in_=sqs[32:64, 512:1024]), "krx", 16)
            p.wait("vector", ("krx", tdm))
            _ = vv and vv.tensor_tensor(sqs[0:32, 1024:1536], sqs[0:32, 0:512],
                                        cos4[0:32, sl], op=ALU.mult)
            _ = vv and vv.tensor_tensor(sqs[0:32, 1536:2048], sqs[0:32, 512:1024],
                                        sin4[0:32, sl], op=ALU.mult)
            _ = vv and vv.tensor_sub(krot_sb[0:32, sl], sqs[0:32, 1024:1536],
                                     sqs[0:32, 1536:2048])
            _ = vv and vv.tensor_tensor(sqs[0:32, 1024:1536], sqs[0:32, 512:1024],
                                        cos4[0:32, sl], op=ALU.mult)
            _ = vv and vv.tensor_tensor(sqs[0:32, 1536:2048], sqs[0:32, 0:512],
                                        sin4[0:32, sl], op=ALU.mult)
            _ = vv and vv.tensor_add(kr2[0:32, 0:512], sqs[0:32, 1024:1536],
                                     sqs[0:32, 1536:2048])
            tko = p.dinc("vector", "vd")
            p.wait("gpsimd", ("vd", tko))
            tdm = p.inc(g2 and g2.dma_start(out=krot_sb[32:64, sl], in_=kr2[0:32, 0:512]),
                        "krx", 16)
        p.wait("vector", ("krx", tdm))
        krot_tok = ("vd", p.dinc("vector", "vd"))
        if upto == "krot":
            p.wait("sync", krot_tok)
            return
        fcs["ps_free"] = [krot_tok, krot_tok]
        fcs["dma_free"][0] = [fcs["dma_free"][0], ("sd", tkc)]
        fcs["cvt_free"][0] = [fcs["cvt_free"][0], ("td", tkm)]

        stg_free = [None, None]

        def rhs_q(kc, tc):
            return qa_bf[:, 2048 * kc + 512 * tc:2048 * kc + 512 * (tc + 1)]

        def evac_qp(j, ps_list, tdtok):
            b = j % 2
            v2 = p.e("vector")
            p.wait("vector", tdtok)
            p.wait("vector", stg_free[b])
            for tc, ps in enumerate(ps_list):
                _ = v2 and v2.tensor_copy(stg[b][:, 512 * tc:512 * (tc + 1)], ps)
            tvv = p.dinc("vector", "vd")
            g2 = p.e("gpsimd")
            p.wait("gpsimd", ("vd", tvv))
            tst = p.inc(g2 and g2.dma_start(out=qs[128 * j:128 * (j + 1), :],
                                            in_=stg[b]), "sS%d" % b, 16)
            stg_free[b] = ("sS%d" % b, tst)
            return ("vd", tvv)

        emit_fc(p, C, psum, wt_dram=wqb, K=1536, NTILES=16, rhs_fn=rhs_q,
                rhs_tok=qn_tok, Tn=2048, tw=512, evac=evac_qp, lay=lay_fc, fcs=fcs)

        rot_free = None
        for gp in range(4):
            for half in range(2):
                jj = 16 + 2 * gp + half
                s_ = p.e("sync")
                sl_f = C.at(0 + half * 6144, 128, 1536, F32)
                sl_b = C.at(16384 + half * 3072, 128, 1536, BF16)
                if gp == 0 and half == 0:
                    p.wait("sync", p.tok("sd"))
                else:
                    p.wait("sync", rot_free)
                src_ap = bass.AP(wqb.tensor, 128 * jj,
                                 [[3072, 128], [128 * 3072, 12], [1, 128]])
                twk = p.inc(s_ and s_.dma_start(out=sl_f, in_=src_ap),
                            "rt%d" % half, 16)
                sc2 = p.e("scalar")
                p.wait("scalar", ("rt%d" % half, twk))
                if gp == 0 and half == 0:
                    p.wait("scalar", p.tok("td"))
                _ = sc2 and sc2.copy(sl_b, sl_f)
                tck = p.dinc("scalar", "sd")
                t = p.e("tensor")
                p.wait("tensor", ("sd", tck))
                if gp == 0 and half == 0:
                    p.wait("tensor", fcs["ps_free"][0])
                    p.wait("tensor", fcs["ps_free"][1])
                    p.wait("tensor", rot_free)
                last = None
                for kc in range(12):
                    for tc in range(4):
                        last = t and t.matmul(
                            psum[4 * half + tc][:, 0:512], sl_b[:, 128 * kc:128 * (kc + 1)],
                            qa_bf[:, 2048 * kc + 512 * tc:2048 * kc + 512 * (tc + 1)],
                            start=(kc == 0), stop=(kc == 11))
                tdk = p.inc(last, "td", 1)
            v2 = p.e("vector")
            p.wait("vector", ("td", tdk))
            p.wait("vector", rot_free)
            for tc in range(4):
                sl = slice(512 * tc, 512 * (tc + 1))
                p1 = psum[tc][:, 0:512]; p2 = psum[4 + tc][:, 0:512]
                _ = v2 and v2.tensor_tensor(rsc[:, 0:512], p1, cos4[:, sl], op=ALU.mult)
                _ = v2 and v2.tensor_tensor(sqs[:, 0:512], p2, sin4[:, sl], op=ALU.mult)
                _ = v2 and v2.tensor_sub(rstage[0][:, sl], rsc[:, 0:512], sqs[:, 0:512])
                _ = v2 and v2.tensor_tensor(rsc[:, 0:512], p2, cos4[:, sl], op=ALU.mult)
                _ = v2 and v2.tensor_tensor(sqs[:, 0:512], p1, sin4[:, sl], op=ALU.mult)
                _ = v2 and v2.tensor_add(rstage[1][:, sl], rsc[:, 0:512], sqs[:, 0:512])
            trp = p.dinc("vector", "vd")
            g2 = p.e("gpsimd")
            p.wait("gpsimd", ("vd", trp))
            tqr = 0
            for hh in range(4):
                h = 4 * gp + hh
                tqr = p.inc(g2 and g2.dma_start(
                    out=qrs[64 * h:64 * h + 32, :], in_=rstage[0][32 * hh:32 * (hh + 1), :]),
                    "qrS", 16)
                tqr = p.inc(g2 and g2.dma_start(
                    out=qrs[64 * h + 32:64 * h + 64, :], in_=rstage[1][32 * hh:32 * (hh + 1), :]),
                    "qrS", 16)
            rot_free = ("qrS", tqr)

        def rhs_k(kc, tc):
            return kn[:, 2048 * kc + 512 * tc:2048 * kc + 512 * (tc + 1)]

        def evac_kp(j, ps_list, tdtok):
            b = j % 2
            v2 = p.e("vector")
            p.wait("vector", tdtok)
            p.wait("vector", stg_free[b])
            for tc, ps in enumerate(ps_list):
                _ = v2 and v2.tensor_copy(stg[b][:, 512 * tc:512 * (tc + 1)], ps)
            tvv = p.dinc("vector", "vd")
            g2 = p.e("gpsimd")
            p.wait("gpsimd", ("vd", tvv))
            tst = p.inc(g2 and g2.dma_start(out=ks[128 * j:128 * (j + 1), :],
                                            in_=stg[b]), "sS%d" % b, 16)
            stg_free[b] = ("sS%d" % b, tst)
            return ("vd", tvv)

        fcs["dma_free"] = [rot_free] * 4
        fcs["cvt_free"] = [rot_free] * 4
        fcs["ps_free"] = [rot_free, rot_free]
        emit_fc(p, C, psum, wt_dram=wkvb, K=512, NTILES=16, rhs_fn=rhs_k,
                rhs_tok=kn_tok, Tn=2048, tw=512, evac=evac_kp, lay=lay_fc, fcs=fcs)

        wvt_v = [C.at(0 + dc * 4096, 128, 2048, BF16) for dc in range(4)]
        wvf2 = C.at(16384, 128, 2048, F32)
        kpmm_done = p.tok("td")
        last_cvt = None
        for dc in range(4):
            s2 = p.e("sync")
            if dc == 0:
                p.wait("sync", kpmm_done)
                p.wait("sync", fcs["dma_free"][0])
                p.wait("sync", fcs["dma_free"][1])
                p.wait("sync", fcs["dma_free"][2])
                p.wait("sync", fcs["dma_free"][3])
            else:
                p.wait("sync", ("sd", last_cvt))
            tw2 = p.inc(s2 and s2.dma_start(out=wvf2,
                                            in_=wkvb[128 * dc:128 * (dc + 1), 2048:4096]),
                        "wvl", 16)
            sc2 = p.e("scalar")
            p.wait("scalar", ("wvl", tw2))
            if dc == 0:
                p.wait("scalar", fcs["cvt_free"][0])
                p.wait("scalar", fcs["cvt_free"][1])
            _ = sc2 and sc2.copy(wvt_v[dc], wvf2)
            last_cvt = p.dinc("scalar", "sd")
        wv_tok = ("sd", last_cvt)

        vps_free = [fcs["ps_free"][0], fcs["ps_free"][1]]
        for tt in range(16):
            t = p.e("tensor")
            p.wait("tensor", wv_tok)
            p.wait("tensor", vps_free[tt % 2])
            last = None
            for dc in range(4):
                for vc in range(4):
                    last = t and t.matmul(psum[4 * (tt % 2) + vc][:, 0:512],
                                          kn[:, 2048 * dc + 128 * tt:2048 * dc + 128 * (tt + 1)],
                                          wvt_v[dc][:, 512 * vc:512 * (vc + 1)],
                                          start=(dc == 0), stop=(dc == 3))
            tdk = p.inc(last, "td", 1)
            b = tt % 2
            v2 = p.e("vector")
            p.wait("vector", ("td", tdk))
            p.wait("vector", stg_free[b])
            for vc in range(4):
                _ = v2 and v2.tensor_copy(stg[b][:, 512 * vc:512 * (vc + 1)],
                                          psum[4 * b + vc][:, 0:512])
            tvv = p.dinc("vector", "vd")
            vps_free[b] = ("vd", tvv)
            g2 = p.e("gpsimd")
            p.wait("gpsimd", ("vd", tvv))
            tst = p.inc(g2 and g2.dma_start(out=vs[128 * tt:128 * (tt + 1), :],
                                            in_=stg[b]), "vS%d" % b, 16)
            stg_free[b] = ("vS%d" % b, tst)
        v_done = [p.tok("vS0"), p.tok("vS1"), p.tok("sS0"), p.tok("sS1"),
                  p.tok("qrS")]
        if upto == "proj":
            p.wait("sync", v_done)
            return

        head_free = v_done + [mask_tok, krot_tok]
        at_done = emit_attention(p, C, psum, nheads=16,
                                 sc=1.0 / float(np.sqrt(192.0)),
                                 q_dram=qs, k_dram=ks, v_dram=vs, nrow_d=nrow,
                                 lrow_d=lrow, o_sb=o_sb, lay=lay_at,
                                 masks=(mdiag, m1s), tag="at",
                                 qr_dram=qrs, krot_sb=krot_sb,
                                 head_free_tok=head_free)

        fcs["dma_free"] = [at_done] * 4
        fcs["cvt_free"] = [at_done] * 4
        fcs["ps_free"] = [at_done, at_done]
        xi_free = [at_done, at_done]
        ost_free = [at_done, at_done]

        def rhs_o(kc, tc):
            return o_sb[:, 2048 * kc + 512 * tc:2048 * kc + 512 * (tc + 1)]

        def evac_o(j, ps_list, tdtok):
            b = j % 2
            g2 = p.e("gpsimd")
            p.wait("gpsimd", xi_free[b])
            txi = p.inc(g2 and g2.dma_start(out=xit[b],
                                            in_=xi[128 * j:128 * (j + 1), :]),
                        "xi%d" % b, 16)
            v2 = p.e("vector")
            p.wait("vector", tdtok)
            p.wait("vector", ("xi%d" % b, txi))
            p.wait("vector", ost_free[b])
            for tc, ps in enumerate(ps_list):
                _ = v2 and v2.tensor_add(ost[b][:, 512 * tc:512 * (tc + 1)], ps,
                                         xit[b][:, 512 * tc:512 * (tc + 1)])
            tvv = p.dinc("vector", "vd")
            xi_free[b] = ("vd", tvv)
            g3 = p.e("gpsimd")
            p.wait("gpsimd", ("vd", tvv))
            tst = p.inc(g3 and g3.dma_start(out=xo[128 * j:128 * (j + 1), :],
                                            in_=ost[b]), "oS%d" % b, 16)
            ost_free[b] = ("oS%d" % b, tst)
            return ("vd", tvv)

        emit_fc(p, C, psum, wt_dram=wo0, K=2048, NTILES=16, rhs_fn=rhs_o,
                rhs_tok=at_done, Tn=2048, tw=512, evac=evac_o, lay=lay_fc, fcs=fcs)

        p.wait("sync", [p.tok("oS0"), p.tok("oS1")])

    return emit


def build_launch1(dbg=False, upto='full'):
    nc = bass.Bass("TRN2", target_bir_lowering=False)
    xi = nc.dram_tensor("xi", [D, T], F32, kind="ExternalInput").ap()
    xo = nc.dram_tensor("xo", [D, T], F32, kind="ExternalOutput").ap()
    return build_engines(nc, make_launch1(nc, xi, xo, "", dbg, upto))


def run_launch1(d, xs, dbg=False, upto='full'):
    nc = build_launch1(dbg, upto)
    in_maps = [dict(xi=xs[c], wqa=d["wqa"], qnw=d["qnw"], wqb=d["wqb"],
                    wkr=d["wkr"], wkva=d["wkva"], kvnw=d["kvnw"], wkvb=d["wkvb"],
                    wo0=d["wo0"], cos4=d["cos4b"], sin4=d["sin4b"],
                    mdiag=d["mdiag"], m1s=d["m1s"]) for c in range(len(xs))]
    res = run_bass_kernel_spmd(nc, in_maps, core_ids=list(range(len(xs))))
    return res.results




def make_launch4(nc, xi, xo, pref='', dbg=False):
    gwp = nc.dram_tensor(pref + "gwp", [1024, 16384], I32, kind="ExternalInput").ap()
    gsc = nc.dram_tensor(pref + "gsc", [64, 16384], F32, kind="ExternalInput").ap()
    uwp = nc.dram_tensor(pref + "uwp", [1024, 16384], I32, kind="ExternalInput").ap()
    usc = nc.dram_tensor(pref + "usc", [64, 16384], F32, kind="ExternalInput").ap()
    dwp = nc.dram_tensor(pref + "dwp", [8192, 2048], I32, kind="ExternalInput").ap()
    dsc = nc.dram_tensor(pref + "dsc", [512, 2048], F32, kind="ExternalInput").ap()
    xln_st = nc.dram_tensor(pref + "xln_st", [D, T], BF16, kind="Internal").ap()
    wgd = nc.dram_tensor(pref + "wgd", [2048, 16384], BF16, kind="Internal").ap()
    wud = nc.dram_tensor(pref + "wud", [2048, 16384], BF16, kind="Internal").ap()
    wdd = nc.dram_tensor(pref + "wdd", [16384, 2048], BF16, kind="Internal").ap()
    rows_d = nc.dram_tensor(pref + "rows_d", [2, 2048], F32, kind="Internal").ap()

    KB = 1024

    def emit(p, C, psum):
        lay_ln = dict(xt=90112, sq=106496, mB=122880, rB=131072, s16=139264,
                      ones=139648)
        xlns = [C.at(139776 + k * 4096, 128, 2048, BF16) for k in range(2)]
        stage_free = [None, None]

        def ln_out(k, xt_ap, mB, rB, tok):
            b = k % 2
            v = p.e("vector")
            p.wait("vector", tok)
            p.wait("vector", stage_free[b])
            _ = v and v.scalar_tensor_tensor(xt_ap, xt_ap, 1.0, mB, op0=ALU.mult,
                                             op1=ALU.subtract)
            _ = v and v.tensor_tensor(xlns[b], xt_ap, rB, op=ALU.mult)
            tvv = p.dinc("vector", "vd")
            g2 = p.e("gpsimd")
            p.wait("gpsimd", ("vd", tvv))
            tst = p.inc(g2 and g2.dma_start(out=xln_st[128 * k:128 * (k + 1), :],
                                            in_=xlns[b]), "xs%d" % b, 16)
            stage_free[b] = ("xs%d" % b, tst)
            return ("vd", tvv)

        emit_ln(p, C, psum, x_dram=xi, rows_dram=rows_d, out_cb=ln_out,
                lay=lay_ln, tag="ln")
        ln_done = [p.tok("xs0"), p.tok("xs1")]

        pk = [C.at(i * 8192, 128, 2048, I32) for i in range(2)]
        scs = [C.at(86016 + i * 8192, 128, 2048, F32) for i in range(2)]
        cl = C.at(16384, 128, 2048, I32)
        ch = C.at(24576, 128, 2048, I32)
        a_ = C.at(32768, 128, 2048, I32)
        b_ = C.at(40960, 128, 2048, I32)
        u_ = C.at(49152, 128, 2048, I32)
        w_ = C.at(57344, 128, 2048, I32)
        bits = C.at(65536, 128, 2048, U16)
        wlo = C.at(69632, 128, 2048, BF16)
        whi = C.at(73728, 128, 2048, BF16)

        def dq_nib(v, c_ap, out_ap, sc_ap):
            _ = v and v.tensor_scalar(a_, c_ap, 8, 12, op0=ALU.bitwise_and,
                                      op1=ALU.logical_shift_left)
            _ = v and v.tensor_scalar(b_, c_ap, 7, 6, op0=ALU.bitwise_and,
                                      op1=ALU.logical_shift_left)
            _ = v and v.scalar_tensor_tensor(u_, b_, 252, a_, op0=ALU.mult, op1=ALU.add)
            _ = v and v.scalar_tensor_tensor(w_, b_, 16128, a_, op0=ALU.add, op1=ALU.add)
            _ = v and v.tensor_tensor(bits, u_, w_, op=ALU.min)
            _ = v and v.tensor_tensor(out_ap, bits.bitcast(BF16), sc_ap, op=ALU.mult)

        dqs = dict(pk=[ln_done, ln_done], sc=[ln_done, ln_done], out=[None, None])

        def dq_matrix(wp, st, wout, K2, N, tag):
            nslab = K2 // 128
            nch = N // 2048
            pk_tok = dqs["pk"]
            sc_tok = dqs["sc"]
            out_tok = dqs["out"]
            dqv = 0
            for s in range(nslab):
                for c in range(nch):
                    b = (s * nch + c) % 2
                    n0 = 2048 * c
                    sy = p.e("sync")
                    p.wait("sync", pk_tok[b])
                    src = bass.AP(wp.tensor, 128 * s * N + n0, [[N, 128], [1, 2048]])
                    tpk = p.inc(sy and sy.dma_start(out=pk[b], in_=src),
                                tag + "p%d" % b, 16)
                    p.wait("sync", sc_tok[b])
                    ssrc = bass.AP(st.tensor, 8 * s * N + n0,
                                   [[N, 8], [0, 16], [1, 2048]])
                    tsc = p.inc(sy and sy.dma_start(out=scs[b], in_=ssrc),
                                tag + "s%d" % b, 16)
                    v = p.e("vector")
                    p.wait("vector", (tag + "p%d" % b, tpk))
                    p.wait("vector", (tag + "s%d" % b, tsc))
                    p.wait("vector", out_tok[0])
                    p.wait("vector", out_tok[1])
                    _ = v and v.tensor_scalar(cl, pk[b], 15, None, op0=ALU.bitwise_and)
                    _ = v and v.tensor_scalar(ch, pk[b], 4, None,
                                              op0=ALU.logical_shift_right)
                    pk_tok[b] = ("vd", p.dinc("vector", "vd"))
                    dq_nib(v, cl, wlo, scs[b])
                    dq_nib(v, ch, whi, scs[b])
                    dqv = p.dinc("vector", "vd")
                    sc_tok[b] = ("vd", dqv)
                    g2 = p.e("gpsimd")
                    p.wait("gpsimd", ("vd", dqv))
                    dlo = bass.AP(wout.tensor, (256 * s) * N + n0, [[2 * N, 128], [1, 2048]])
                    dhi = bass.AP(wout.tensor, (256 * s + 1) * N + n0,
                                  [[2 * N, 128], [1, 2048]])
                    p.inc(g2 and g2.dma_start(out=dlo, in_=wlo), tag + "ol", 16)
                    tqh = p.inc(g2 and g2.dma_start(out=dhi, in_=whi), tag + "oh", 16)
                    out_tok[0] = (tag + "ol", p.cnt[tag + "ol"])
                    out_tok[1] = (tag + "oh", tqh)
            return [p.tok(tag + "ol"), p.tok(tag + "oh")]

        dq1 = dq_matrix(gwp, gsc, wgd, 1024, 16384, "dq")
        dq2 = dq_matrix(uwp, usc, wud, 1024, 16384, "dq")
        dq3 = dq_matrix(dwp, dsc, wdd, 8192, 2048, "dq")
        dq_done = dq1 + dq2 + dq3

        xh = C.at(24576, 128, 16 * KB, BF16)
        wds = [C.at(57344 + i * 4096, 128, 2048, BF16) for i in range(2)]
        inter = C.at(65536, 128, 8 * KB, BF16)
        g1s = C.at(81920, 128, KB, F32)
        g2s = C.at(86016, 128, KB, F32)
        acc = C.at(90112, 128, 16 * KB, F32)
        xit = [C.at(155648 + i * 4096, 128, KB, F32) for i in range(2)]
        ost = [C.at(163840 + i * 4096, 128, KB, F32) for i in range(2)]
        gsl = [C.at(172032 + i * 4096, 128, 2048, BF16) for i in range(3)]
        usl = [C.at(184320 + i * 4096, 128, 2048, BF16) for i in range(3)]

        NSL = 3
        gsl_tok = [dq_done] * NSL
        usl_tok = [dq_done] * NSL
        gsl_dma = [None] * NSL
        usl_dma = [None] * NSL
        wds_tok = [dq_done, dq_done]
        gu_ps = None
        dn_ps = [None, None]
        acc_tok = [None] * 16
        gidx = 0

        for hf in range(2):
            c0 = KB * hf
            g2 = p.e("gpsimd")
            p.wait("gpsimd", ln_done if hf == 0 else acc_tok[15])
            if hf == 0:
                p.wait("gpsimd", dq_done)
            xsem = "xh%d" % hf
            txh = 0
            for kc in range(16):
                txh = p.inc(g2 and g2.dma_start(
                    out=xh[:, KB * kc:KB * (kc + 1)],
                    in_=xln_st[128 * kc:128 * (kc + 1), c0:c0 + KB]), xsem, 16)
            xh_tok = (xsem, txh)
            p.wait("tensor", xh_tok)
            for nb in range(16):
                inter_tok = [None] * 8
                for j in range(8):
                    jj = 128 * (8 * nb + j)
                    su = gidx % NSL
                    gidx += 1
                    sy = p.e("sync")
                    p.wait("sync", gsl_tok[su])
                    src = bass.AP(wgd.tensor, jj, [[16384, 128], [128 * 16384, 16], [1, 128]])
                    tg_ = p.inc(sy and sy.dma_start(out=gsl[su], in_=src), "gw%d" % su, 16)
                    gsl_dma[su] = ("gw%d" % su, tg_)
                    p.wait("sync", usl_tok[su])
                    src = bass.AP(wud.tensor, jj, [[16384, 128], [128 * 16384, 16], [1, 128]])
                    tu_ = p.inc(sy and sy.dma_start(out=usl[su], in_=src), "uw%d" % su, 16)
                    usl_dma[su] = ("uw%d" % su, tu_)
                    gb = 4 * (j % 2)
                    t = p.e("tensor")
                    p.wait("tensor", gsl_dma[su])
                    p.wait("tensor", usl_dma[su])
                    p.wait("tensor", gu_ps)
                    if j == 0:
                        p.wait("tensor", dn_ps[0])
                        p.wait("tensor", dn_ps[1])
                    last = None
                    for kc in range(16):
                        for tc in range(2):
                            last = t and t.matmul(
                                psum[gb + tc][:, 0:512], gsl[su][:, 128 * kc:128 * (kc + 1)],
                                xh[:, KB * kc + 512 * tc:KB * kc + 512 * (tc + 1)],
                                start=(kc == 0), stop=(kc == 15))
                    for kc in range(16):
                        for tc in range(2):
                            last = t and t.matmul(
                                psum[gb + 2 + tc][:, 0:512], usl[su][:, 128 * kc:128 * (kc + 1)],
                                xh[:, KB * kc + 512 * tc:KB * kc + 512 * (tc + 1)],
                                start=(kc == 0), stop=(kc == 15))
                    tdk = p.inc(last, "td", 1)
                    gsl_tok[su] = ("td", tdk)
                    usl_tok[su] = ("td", tdk)
                    s = p.e("scalar")
                    p.wait("scalar", ("td", tdk))
                    for tc in range(2):
                        _ = s and s.activation(g1s[:, 512 * tc:512 * (tc + 1)],
                                               psum[gb + tc][:, 0:512], AF.Square)
                    ts1 = p.dinc("scalar", "sd")
                    v = p.e("vector")
                    p.wait("vector", ("sd", ts1))
                    for tc in range(2):
                        _ = v and v.tensor_copy(g2s[:, 512 * tc:512 * (tc + 1)],
                                                psum[gb + tc][:, 0:512])
                    _ = v and v.tensor_scalar(g1s, g1s, 0.044715, 1.0, op0=ALU.mult,
                                              op1=ALU.add)
                    _ = v and v.tensor_mul(g1s, g1s, g2s)
                    tvv = p.dinc("vector", "vd")
                    p.wait("scalar", ("vd", tvv))
                    _ = s and s.activation(g1s, g1s, AF.Tanh, scale=0.7978845608028654)
                    ts2 = p.dinc("scalar", "sd")
                    p.wait("vector", ("sd", ts2))
                    _ = v and v.tensor_scalar(g1s, g1s, 0.5, 0.5, op0=ALU.mult,
                                              op1=ALU.add)
                    for tc in range(2):
                        _ = v and v.tensor_tensor(g2s[:, 512 * tc:512 * (tc + 1)],
                                                  g2s[:, 512 * tc:512 * (tc + 1)],
                                                  psum[gb + 2 + tc][:, 0:512],
                                                  op=ALU.mult)
                    _ = v and v.tensor_tensor(inter[:, KB * j:KB * (j + 1)], g2s, g1s,
                                              op=ALU.mult)
                    tin = p.dinc("vector", "vd")
                    inter_tok[j] = ("vd", tin)
                    gu_ps = ("vd", tin)
                for nc_ in range(8):
                    b = nc_ % 2
                    rows = 1024 * nb + 128 * nc_
                    sy = p.e("sync")
                    p.wait("sync", wds_tok[b])
                    tdw = p.inc(sy and sy.dma_start(
                        out=wds[b], in_=wdd[rows:rows + 128, :]), "dw%d" % b, 16)
                    t = p.e("tensor")
                    p.wait("tensor", ("dw%d" % b, tdw))
                    p.wait("tensor", inter_tok[nc_])
                    if nc_ == 0:
                        p.wait("tensor", inter_tok[7])
                    last = None
                    for ht in range(16):
                        hb2 = 4 * (ht % 2)
                        if nc_ == 0:
                            p.wait("tensor", acc_tok[ht])
                        p.wait("tensor", dn_ps[ht % 2])
                        last = None
                        for tc in range(2):
                            last = t and t.matmul(
                                psum[hb2 + tc][:, 0:512],
                                wds[b][:, 128 * ht:128 * (ht + 1)],
                                inter[:, KB * nc_ + 512 * tc:KB * nc_ + 512 * (tc + 1)],
                                start=True, stop=True, skip_group_check=True)
                        tdd = p.inc(last, "td", 1)
                        v = p.e("vector")
                        p.wait("vector", ("td", tdd))
                        for tc in range(2):
                            if nb == 0 and nc_ == 0:
                                _ = v and v.tensor_copy(
                                    acc[:, KB * ht + 512 * tc:KB * ht + 512 * (tc + 1)],
                                    psum[hb2 + tc][:, 0:512])
                            else:
                                _ = v and v.tensor_tensor(
                                    acc[:, KB * ht + 512 * tc:KB * ht + 512 * (tc + 1)],
                                    acc[:, KB * ht + 512 * tc:KB * ht + 512 * (tc + 1)],
                                    psum[hb2 + tc][:, 0:512], op=ALU.add)
                        tac = p.dinc("vector", "vd")
                        acc_tok[ht] = ("vd", tac)
                        dn_ps[ht % 2] = ("vd", tac)
                    wds_tok[b] = ("vd", tac)
            xi_free = [None, None]
            ost_free = [None, None]
            for ht in range(16):
                b = ht % 2
                g3 = p.e("gpsimd")
                p.wait("gpsimd", xi_free[b])
                txi = p.inc(g3 and g3.dma_start(
                    out=xit[b], in_=xi[128 * ht:128 * (ht + 1), c0:c0 + KB]),
                    "fx%d" % b, 16)
                v = p.e("vector")
                p.wait("vector", acc_tok[ht])
                p.wait("vector", ("fx%d" % b, txi))
                p.wait("vector", ost_free[b])
                _ = v and v.tensor_add(ost[b], acc[:, KB * ht:KB * (ht + 1)], xit[b])
                tvv = p.dinc("vector", "vd")
                xi_free[b] = ("vd", tvv)
                g4 = p.e("gpsimd")
                p.wait("gpsimd", ("vd", tvv))
                tst = p.inc(g4 and g4.dma_start(
                    out=xo[128 * ht:128 * (ht + 1), c0:c0 + KB], in_=ost[b]),
                    "fo%d" % b, 16)
                ost_free[b] = ("fo%d" % b, tst)
                acc_tok[ht] = ("vd", tvv)

        p.wait("sync", [p.tok("fo0"), p.tok("fo1")])

    return emit


def build_launch4(dbg=False):
    nc = bass.Bass("TRN2", target_bir_lowering=False)
    xi = nc.dram_tensor("xi", [D, T], F32, kind="ExternalInput").ap()
    xo = nc.dram_tensor("xo", [D, T], F32, kind="ExternalOutput").ap()
    return build_engines(nc, make_launch4(nc, xi, xo, "", dbg))


def run_launch4(d, xs, dbg=False):
    nc = build_launch4(dbg)
    in_maps = [dict(xi=xs[c], gwp=d["gwp"], gsc=d["gsc"], uwp=d["uwp"],
                    usc=d["usc"], dwp=d["dwp"], dsc=d["dsc"])
               for c in range(len(xs))]
    res = run_bass_kernel_spmd(nc, in_maps, core_ids=list(range(len(xs))))
    return res.results



def build_fused():
    nc = bass.Bass("TRN2", target_bir_lowering=False)
    x0 = nc.dram_tensor("xi", [D, T], F32, kind="ExternalInput").ap()
    x1 = nc.dram_tensor("x1", [D, T], F32, kind="Internal").ap()
    x2 = nc.dram_tensor("x2", [D, T], F32, kind="Internal").ap()
    x3 = nc.dram_tensor("x3", [D, T], F32, kind="Internal").ap()
    out = nc.dram_tensor("xo", [D, T], F32, kind="ExternalOutput").ap()
    e1 = make_launch1(nc, x0, x1, "L1")
    e2 = make_launch2(nc, x1, x2, "L2")
    e3 = make_launch3(nc, x2, x3, "L3")
    e4 = make_launch4(nc, x3, out, "L4")

    def bar(p, toks):
        for e in ENGS:
            p.wait(e, toks)

    def emit(p, C, psum):
        e1(p, C, psum)
        bar(p, [p.tok("oS0"), p.tok("oS1")])
        e2(p, C, psum)
        bar(p, [p.tok("os0"), p.tok("os1")])
        e3(p, C, psum)
        bar(p, [p.tok("oS0"), p.tok("oS1")])
        e4(p, C, psum)

    return build_engines(nc, emit)


def fused_in_map(d):
    return {
        "L1wqa": d["wqa"], "L1qnw": d["qnw"], "L1wqb": d["wqb"], "L1wkr": d["wkr"],
        "L1wkva": d["wkva"], "L1kvnw": d["kvnw"], "L1wkvb": d["wkvb"],
        "L1wo0": d["wo0"], "L1cos4": d["cos4b"], "L1sin4": d["sin4b"],
        "L1mdiag": d["mdiag"], "L1m1s": d["m1s"],
        "L2w1t": d["w1t"], "L2b1c": d["b1c"], "L2w2t": d["w2t"], "L2b2c": d["b2c"],
        "L2w3t": d["w3t"], "L2b3c": d["b3c"],
        "L3wq1": d["wq1"], "L3wk1": d["wk1"], "L3wv1": d["wv1"], "L3wo1": d["wo1"],
        "L3mdiag": d["mdiag"], "L3m1s": d["m1s"],
        "L4gwp": d["gwp"], "L4gsc": d["gsc"], "L4uwp": d["uwp"], "L4usc": d["usc"],
        "L4dwp": d["dwp"], "L4dsc": d["dsc"],
    }



def kernel(**inputs):
    d = prep_host(inputs)
    xs = d["_xcores"]
    nc = build_fused()
    base = fused_in_map(d)
    in_maps = [dict(base, xi=xs[c]) for c in range(NCORE)]
    res = run_bass_kernel_spmd(nc, in_maps, core_ids=list(range(NCORE)))
    out = np.stack([r["xo"].T for r in res.results])
    return np.ascontiguousarray(out)


def kernel_unfused(**inputs):
    d = prep_host(inputs)
    xs = d["_xcores"]
    r1 = run_launch1(d, xs)
    x1 = [np.ascontiguousarray(r["xo"]) for r in r1]
    r2 = run_launch2(d, x1)
    x2 = [np.ascontiguousarray(r["xo"]) for r in r2]
    r3 = run_launch3(d, x2)
    x3 = [np.ascontiguousarray(r["xo"]) for r in r3]
    r4 = run_launch4(d, x3)
    out = np.stack([r["xo"].T for r in r4])
    return np.ascontiguousarray(out)


# version 03 (final)
# speedup vs baseline: 1.2197x; 1.2197x over previous
import numpy as np
import ml_dtypes
import concourse.bass as bass
import concourse.mybir as mybir
from concourse.bass_utils import run_bass_kernel_spmd
from contextlib import ExitStack

F32 = mybir.dt.float32
BF16 = mybir.dt.bfloat16
I32 = mybir.dt.int32
U16 = mybir.dt.uint16
U8 = mybir.dt.uint8
AF = mybir.ActivationFunctionType
ALU = mybir.AluOpType
AX = mybir.AxisListType

NCORE = 8
T = 2048
D = 2048
NEG = np.float32(-3.4028235e38)
ENGS = ("sync", "tensor", "vector", "scalar", "gpsimd")
SBUF_BYTES = 207 * 1024



def _bcast_chunks(v, p=128):
    return np.ascontiguousarray(np.asarray(v).reshape(-1, p).T)


def prep_host(inputs):
    i = {k: np.asarray(v) for k, v in inputs.items()}
    H = 16
    d = {}
    x = i["L0_attn_hidden_states"]
    d["_xcores"] = [np.ascontiguousarray(x[c].T) for c in range(NCORE)]

    d["wqa"] = np.ascontiguousarray(i["L0_attn_Wq_a"])
    d["qnw"] = _bcast_chunks(i["L0_attn_q_norm_w"])
    wqb = i["L0_attn_Wq_b"]
    perm = []
    for h in range(H):
        perm.extend(h * 192 + j for j in range(128))
    for g in range(4):
        for half in range(2):
            for hh in range(4):
                perm.extend((4 * g + hh) * 192 + 128 + half * 32 + j for j in range(32))
    d["wqb"] = np.ascontiguousarray(wqb[:, perm])
    d["wkr"] = np.ascontiguousarray(i["L0_attn_Wk_rope"])
    d["wkva"] = np.ascontiguousarray(i["L0_attn_Wkv_a"])
    d["kvnw"] = _bcast_chunks(i["L0_attn_kv_norm_w"])
    wkvb = i["L0_attn_Wkv_b"]
    perm2 = [h * 256 + j for h in range(H) for j in range(128)] + \
            [h * 256 + 128 + j for h in range(H) for j in range(128)]
    d["wkvb"] = np.ascontiguousarray(wkvb[:, perm2])
    d["wo0"] = np.ascontiguousarray(i["L0_attn_Wo"])

    d["w1t"] = np.ascontiguousarray(i["L0_ffn_W1"].T)
    d["b1c"] = _bcast_chunks(i["L0_ffn_b1"])
    d["w2t"] = np.ascontiguousarray(i["L0_ffn_W2"].T)
    d["b2c"] = _bcast_chunks(i["L0_ffn_b2"])
    d["w3t"] = np.ascontiguousarray(i["L0_ffn_W3"].T)
    d["b3c"] = _bcast_chunks(i["L0_ffn_b3"])

    d["wq1"] = np.ascontiguousarray(i["L1_attn_Wq"].T)
    d["wk1"] = np.ascontiguousarray(i["L1_attn_Wk"].T)
    d["wv1"] = np.ascontiguousarray(i["L1_attn_Wv"].T)
    d["wo1"] = np.ascontiguousarray(i["L1_attn_Wo"].T)

    d["gwp"] = np.ascontiguousarray(i["L1_ffn_gate_W_packed"].T)
    d["gsc"] = np.ascontiguousarray(i["L1_ffn_gate_scales"].T)
    d["uwp"] = np.ascontiguousarray(i["L1_ffn_up_W_packed"].T)
    d["usc"] = np.ascontiguousarray(i["L1_ffn_up_scales"].T)
    d["dwp"] = np.ascontiguousarray(i["L1_ffn_down_W_packed"].T)
    d["dsc"] = np.ascontiguousarray(i["L1_ffn_down_scales"].T)

    inv = 1.0 / 10000.0 ** (np.arange(0, 64, 2, dtype=np.float32) / 64)
    tt = np.arange(T, dtype=np.float32)
    fr = np.outer(tt, inv)
    cos = np.cos(fr).astype(np.float32).T
    sin = np.sin(fr).astype(np.float32).T
    d["cos4"] = np.ascontiguousarray(np.tile(cos, (4, 1)))
    d["sin4"] = np.ascontiguousarray(np.tile(sin, (4, 1)))
    d["cos4b"] = d["cos4"].astype(ml_dtypes.bfloat16)
    d["sin4b"] = d["sin4"].astype(ml_dtypes.bfloat16)
    ii = np.arange(128)
    d["mdiag"] = np.where(ii[None, :] > ii[:, None], NEG, np.float32(0.0)).astype(np.float32)
    m1 = np.zeros((4, 128, 512), np.float32)
    for oi, delta in enumerate((0, 128, 256, 384)):
        jj = np.arange(512)
        m1[oi] = (jj[None, :] >= (ii[:, None] + delta)).astype(np.float32)
    d["m1s"] = np.ascontiguousarray(m1.astype(ml_dtypes.bfloat16))
    return d



class Prog:

    def __init__(self, active, eng, sempool):
        self.active = active
        self._eng = eng
        self.pool = sempool
        self.idx = {}
        self.cnt = {}

    def _sem(self, name):
        if name not in self.idx:
            i = len(self.idx)
            assert i < len(self.pool), "semaphore pool exhausted"
            self.idx[name] = i
            self.cnt[name] = 0
        return self.pool[self.idx[name]]

    def e(self, name):
        return self._eng.get(name) if name == self.active else None

    def inc(self, instr, sem, amt):
        h = self._sem(sem)
        self.cnt[sem] += amt
        if instr is not None:
            instr.then_inc(h, amt)
        return self.cnt[sem]

    def dinc(self, engname, sem, amt=1):
        h = self._sem(sem)
        self.cnt[sem] += amt
        e = self.e(engname)
        if e is not None:
            e.drain().then_inc(h, amt)
        return self.cnt[sem]

    def fence(self, engname, sem):
        v = self.dinc(engname, sem)
        e = self.e(engname)
        if e is not None:
            e.wait_ge(self._sem(sem), v)
        return (sem, v)

    def tok(self, sem):
        self._sem(sem)
        return (sem, self.cnt[sem])

    def wait(self, engname, token):
        if token is None:
            return
        if isinstance(token, list):
            for t in token:
                self.wait(engname, t)
            return
        sem, val = token
        h = self._sem(sem)
        e = self.e(engname)
        if e is not None and val > 0:
            e.wait_ge(h, val)


class Carver:
    def __init__(self, backing_ap, nbytes):
        self.b = backing_ap
        self.n = nbytes

    def at(self, off, p, f, dt):
        sz = f * mybir.dt.size(dt)
        assert off % 4 == 0 and off + sz <= self.n, (off, sz, self.n)
        return self.b[0:p, off:off + sz].bitcast(dt)


NSEMS = 96


def build_engines(nc, emit, sbuf_bytes=SBUF_BYTES):
    with ExitStack() as ctx:
        backing = ctx.enter_context(nc.sbuf_tensor("arena", [128, sbuf_bytes], U8))
        C = Carver(backing.ap(), sbuf_bytes)
        psum = [ctx.enter_context(nc.psum_tensor(f"pb{i}", [128, 512], F32))
                for i in range(8)]
        pool = [ctx.enter_context(nc.semaphore(f"s{i}")) for i in range(NSEMS)]
        block = ctx.enter_context(nc.Block())
        eng = {}

        def section(engname):
            def run(e):
                eng[engname] = e
                p = Prog(engname, eng, pool)
                emit(p, C, psum)
            return run

        block.sync(section("sync"))
        block.tensor(section("tensor"))
        block.vector(section("vector"))
        block.scalar(section("scalar"))
        block.gpsimd(section("gpsimd"))
    return nc



def emit_ln(p, C, psum, *, x_dram, rows_dram, out_cb, lay, tag, upto='full', dbg_dram=None):
    xt = [C.at(lay["xt"] + k * 8192, 128, 2048, F32) for k in range(2)]
    sq = [C.at(lay["sq"] + k * 8192, 128, 2048, F32) for k in range(2)]
    mB = C.at(lay["mB"], 128, 2048, F32)
    rB = C.at(lay["rB"], 128, 2048, F32)
    s16 = C.at(lay["s16"], 128, 64, F32)
    mean16 = s16[:, 0:16]; var16 = s16[:, 16:32]; sv16 = s16[:, 32:48]; rs16 = s16[:, 48:64]
    ones_col = C.at(lay["ones"], 128, 1, F32)
    xsem = [tag + "x0", tag + "x1"]

    ps_cols = psum[6]
    sred = C.at(lay["s16"] + 256, 128, 32, F32)
    v = p.e("vector")
    _ = v and v.memset(ones_col, 1.0)
    ones_tok = ("vd", p.dinc("vector", "vd"))

    xt_free = [None, None]
    sq_free = [None, None]
    tmm = 0
    for k in range(16):
        b = k % 2
        g = p.e("gpsimd")
        p.wait("gpsimd", xt_free[b])
        tload = p.inc(g and g.dma_start(out=xt[b], in_=x_dram[128 * k:128 * (k + 1), :]),
                      xsem[b], 16)
        p.wait("vector", (xsem[b], tload))
        p.wait("vector", sq_free[b])
        vv = p.e("vector")
        _ = vv and vv.tensor_mul(sq[b], xt[b], xt[b])
        tsq = p.dinc("vector", "vd")
        t = p.e("tensor")
        p.wait("tensor", (xsem[b], tload))
        p.wait("tensor", ("vd", tsq))
        if k == 0:
            p.wait("tensor", ones_tok)
        last = None
        for ts in range(16):
            last = t and t.matmul(ps_cols[:, 16 * ts + k:16 * ts + k + 1],
                                  xt[b][:, 128 * ts:128 * (ts + 1)],
                                  ones_col, start=True, stop=True,
                                  skip_group_check=True)
            last = t and t.matmul(ps_cols[:, 256 + 16 * ts + k:256 + 16 * ts + k + 1],
                                  sq[b][:, 128 * ts:128 * (ts + 1)],
                                  ones_col, start=True, stop=True,
                                  skip_group_check=True)
        if k == 15:
            tmm = p.dinc("tensor", "td")
        else:
            tmm = p.inc(last, "td", 1)
        xt_free[b] = ("td", tmm)
        sq_free[b] = ("td", tmm)

    if upto == "stats":
        p.wait("sync", ("td", tmm))
        return
    v = p.e("vector")
    p.wait("vector", ("td", tmm))
    _ = v and v.reduce_sum(sred[:, 0:16], ps_cols[:, 0:256].rearrange(
        "p (a b) -> p a b", b=16), axis=AX.X)
    _ = v and v.reduce_sum(sred[:, 16:32], ps_cols[:, 256:512].rearrange(
        "p (a b) -> p a b", b=16), axis=AX.X)
    p.fence("vector", "vd")
    _ = v and v.tensor_single_scalar(mean16, sred[:, 0:16], 1.0 / D, op=ALU.mult)
    p.fence("vector", "vd")
    _ = v and v.tensor_tensor(sv16, mean16, mean16, op=ALU.mult)
    p.fence("vector", "vd")
    _ = v and v.tensor_single_scalar(sv16, sv16, 1e-5, op=ALU.subtract)
    p.fence("vector", "vd")
    _ = v and v.scalar_tensor_tensor(var16, sred[:, 16:32], 1.0 / D, sv16,
                                     op0=ALU.mult, op1=ALU.subtract)
    tv = p.dinc("vector", "vd")
    p.wait("scalar", ("vd", tv))
    s = p.e("scalar")
    _ = s and s.activation(sv16, var16, AF.Sqrt)
    tsv = p.dinc("scalar", "sd")
    p.wait("vector", ("sd", tsv))
    p.fence("vector", "vd")
    _ = v and v.reciprocal(rs16, sv16)
    trs = p.dinc("vector", "vd")
    if dbg_dram is not None:
        dscr = C.at(lay["xt"], 128, 512, F32)
        _ = v and v.tensor_copy(dscr, psum[6][:, 0:512])
        tdbg = p.dinc("vector", "vd")
        g = p.e("gpsimd")
        p.wait("gpsimd", ("vd", tdbg))
        p.inc(g and g.dma_start(out=dbg_dram[:, 0:512], in_=dscr), tag + "db", 16)
        p.inc(g and g.dma_start(out=dbg_dram[:, 512:544], in_=sred), tag + "db", 16)
        p.inc(g and g.dma_start(out=dbg_dram[:, 544:608], in_=s16), tag + "db", 16)
        p.wait("sync", p.tok(tag + "db"))
    if upto == "rows":
        p.wait("sync", ("vd", trs))
        return
    p.wait("scalar", ("vd", trs))
    sqe = p.e("scalar")
    tg = 0
    rsem = tag + "row"
    for ts in range(16):
        tg = p.inc(sqe and sqe.dma_start(out=rows_dram[0:1, 128 * ts:128 * (ts + 1)],
                                         in_=mean16[:, ts:ts + 1]), rsem, 16)
        tg = p.inc(sqe and sqe.dma_start(out=rows_dram[1:2, 128 * ts:128 * (ts + 1)],
                                         in_=rs16[:, ts:ts + 1]), rsem, 16)
    if upto == "gather":
        p.wait("sync", (rsem, tg))
        return
    g = p.e("gpsimd")
    p.wait("gpsimd", (rsem, tg))
    bsem = tag + "bb"
    mb_src = bass.AP(rows_dram.tensor, 0, [[0, 128], [1, 2048]])
    rb_src = bass.AP(rows_dram.tensor, 2048, [[0, 128], [1, 2048]])
    p.inc(g and g.dma_start(out=mB, in_=mb_src), bsem, 16)
    tbb = p.inc(g and g.dma_start(out=rB, in_=rb_src), bsem, 16)
    last_b = (bsem, tbb)

    if upto == "bcast":
        p.wait("sync", last_b)
        return
    xt_free = [last_b, last_b]
    for k in range(16):
        b = k % 2
        g = p.e("gpsimd")
        p.wait("gpsimd", xt_free[b])
        tload = p.inc(g and g.dma_start(out=xt[b], in_=x_dram[128 * k:128 * (k + 1), :]),
                      xsem[b], 16)
        done = out_cb(k, xt[b], mB, rB, (xsem[b], tload))
        xt_free[b] = done



def emit_fc(p, C, psum, *, wt_dram, K, NTILES, rhs_fn, rhs_tok, Tn, tw, evac,
            lay, fcs, wq="sync", cvt="scalar", split_k=1024, tile_off=0):
    km = K // 128
    tcs = Tn // tw
    assert tcs <= 4
    nsp = (K + split_k - 1) // split_k
    NSLOT = 4
    sfp = [C.at(lay["sf"] + s * split_k * 4, 128, split_k, F32) for s in range(NSLOT)]
    sbp = [C.at(lay["sb"] + s * split_k * 2, 128, split_k, BF16) for s in range(NSLOT)]
    if not fcs:
        fcs.update(dma_free=[None] * NSLOT, cvt_free=[None] * NSLOT,
                   cvt_tok=[None] * NSLOT, ps_free=[None, None], gidx=0)
    csem = "sd" if cvt == "scalar" else "vd"
    nfull = wt_dram.shape[1]

    p.wait("tensor", rhs_tok)
    for j in range(NTILES):
        b = j % 2
        jj = j + tile_off
        pieces = []
        for sp in range(nsp):
            s_ = fcs["gidx"] % NSLOT
            fcs["gidx"] += 1
            k0 = sp * split_k
            kk = min(split_k, K - k0)
            w = p.e(wq)
            p.wait(wq, fcs["dma_free"][s_])
            src_ap = bass.AP(wt_dram.tensor, k0 * nfull + 128 * jj,
                             [[nfull, 128], [128 * nfull, kk // 128], [1, 128]])
            wsem = "fw%d" % s_
            twk = p.inc(w and w.dma_start(out=sfp[s_][:, 0:kk], in_=src_ap), wsem, 16)
            cv = p.e(cvt)
            p.wait(cvt, (wsem, twk))
            p.wait(cvt, fcs["cvt_free"][s_])
            if cvt == "scalar":
                _ = cv and cv.copy(sbp[s_][:, 0:kk], sfp[s_][:, 0:kk])
            else:
                _ = cv and cv.tensor_copy(sbp[s_][:, 0:kk], sfp[s_][:, 0:kk])
            tck = p.dinc(cvt, csem)
            fcs["cvt_tok"][s_] = (csem, tck)
            fcs["dma_free"][s_] = (csem, tck)
            pieces.append(s_)
        t = p.e("tensor")
        p.wait("tensor", fcs["ps_free"][b])
        last = None
        seen = set()
        for kc in range(km):
            sp = (kc * 128) // split_k
            s_ = pieces[sp]
            if sp not in seen:
                seen.add(sp)
                p.wait("tensor", fcs["cvt_tok"][s_])
            kin = kc * 128 - sp * split_k
            lhs = sbp[s_][:, kin:kin + 128]
            for tc in range(tcs):
                last = t and t.matmul(psum[tcs * b + tc][:, 0:tw], lhs,
                                      rhs_fn(kc, tc), start=(kc == 0), stop=(kc == km - 1))
        tdk = p.inc(last, "td", 1)
        for s_ in pieces:
            fcs["cvt_free"][s_] = ("td", tdk)
        ps_list = [psum[tcs * b + tc][:, 0:tw] for tc in range(tcs)]
        fcs["ps_free"][b] = evac(j, ps_list, ("td", tdk))



def make_launch2(nc, xi, xo, pref='', upto='full', dbg=False):
    w1t = nc.dram_tensor(pref + "w1t", [2048, 4096], F32, kind="ExternalInput").ap()
    b1c = nc.dram_tensor(pref + "b1c", [128, 32], F32, kind="ExternalInput").ap()
    w2t = nc.dram_tensor(pref + "w2t", [4096, 4096], F32, kind="ExternalInput").ap()
    b2c = nc.dram_tensor(pref + "b2c", [128, 32], F32, kind="ExternalInput").ap()
    w3t = nc.dram_tensor(pref + "w3t", [4096, 2048], F32, kind="ExternalInput").ap()
    b3c = nc.dram_tensor(pref + "b3c", [128, 16], F32, kind="ExternalInput").ap()
    xln_st = nc.dram_tensor(pref + "xln_st", [D, T], BF16,
                            kind="ExternalOutput" if dbg else "Internal").ap()
    rows_d = nc.dram_tensor(pref + "rows_d", [2, 2048], F32,
                            kind="ExternalOutput" if dbg else "Internal").ap()
    dbg_ln = nc.dram_tensor(pref + "dbg_ln", [128, 1024], F32,
                            kind="ExternalOutput").ap() if dbg else None

    KB = 1024

    def emit(p, C, psum):
        lay_ln = dict(xt=24576, sq=40960, mB=57344, rB=65536, rows=73728,
                      s16=90112, ones=90368)
        lay_fc = dict(sf=0, sb=16384)
        xh = C.at(24576, 128, 16 * KB, BF16)
        h1h = C.at(57344, 128, 32 * KB, BF16)
        h2h = C.at(122880, 128, 32 * KB, BF16)
        MISC = 188416
        b1s = C.at(MISC, 128, 32, F32)
        b2s = C.at(MISC + 128, 128, 32, F32)
        b3s = C.at(MISC + 256, 128, 16, F32)
        xit = [C.at(MISC + 512 + k * 4096, 128, KB, F32) for k in range(2)]
        ost = [C.at(MISC + 8704 + k * 4096, 128, KB, F32) for k in range(2)]
        xlns = [C.at(90624 + k * 4096, 128, 2048, BF16) for k in range(2)]

        g = p.e("gpsimd")
        p.inc(g and g.dma_start(out=b1s, in_=b1c), "bia", 16)
        p.inc(g and g.dma_start(out=b2s, in_=b2c), "bia", 16)
        tb = p.inc(g and g.dma_start(out=b3s, in_=b3c), "bia", 16)
        bias_tok = ("bia", tb)

        stage_free = [None, None]

        def ln_out(k, xt_ap, mB, rB, tok):
            b = k % 2
            v = p.e("vector")
            p.wait("vector", tok)
            p.wait("vector", stage_free[b])
            _ = v and v.scalar_tensor_tensor(xt_ap, xt_ap, 1.0, mB, op0=ALU.mult,
                                             op1=ALU.subtract)
            _ = v and v.tensor_tensor(xlns[b], xt_ap, rB, op=ALU.mult)
            tvv = p.dinc("vector", "vd")
            g2 = p.e("gpsimd")
            p.wait("gpsimd", ("vd", tvv))
            tst = p.inc(g2 and g2.dma_start(out=xln_st[128 * k:128 * (k + 1), :],
                                            in_=xlns[b]), "xs%d" % b, 16)
            stage_free[b] = ("xs%d" % b, tst)
            return ("vd", tvv)

        emit_ln(p, C, psum, x_dram=xi, rows_dram=rows_d, out_cb=ln_out, lay=lay_ln, tag="ln", dbg_dram=dbg_ln,
                upto=upto if upto.startswith(("stats", "rows", "gather", "bcast")) else "full")
        if upto.startswith(("stats", "rows", "gather", "bcast")):
            return
        ln_done = [p.tok("xs0"), p.tok("xs1")]
        if upto == "ln":
            p.wait("sync", ln_done)
            return

        fcs = {}
        cross = {}
        for hf in range(2):
            c0 = KB * hf
            g = p.e("gpsimd")
            p.wait("gpsimd", ln_done if hf == 0 else cross["h1_mm"])
            xsem = "xh%d" % hf
            txh = 0
            for kc in range(16):
                txh = p.inc(g and g.dma_start(
                    out=xh[:, KB * kc:KB * (kc + 1)],
                    in_=xln_st[128 * kc:128 * (kc + 1), c0:c0 + KB]), xsem, 16)
            xh_tok = (xsem, txh)

            def rhs1(kc, tc):
                return xh[:, KB * kc + 512 * tc: KB * kc + 512 * (tc + 1)]

            def evac1(j, ps_list, tdtok):
                s = p.e("scalar")
                p.wait("scalar", tdtok)
                if j == 0:
                    p.wait("scalar", bias_tok)
                    p.wait("scalar", cross.get("h2_mm"))
                for tc, ps in enumerate(ps_list):
                    _ = s and s.activation(
                        h1h[:, KB * j + 512 * tc: KB * j + 512 * (tc + 1)], ps,
                        AF.Relu, bias=b1s[:, j:j + 1])
                return ("sd", p.dinc("scalar", "sd"))

            if hf == 0 and not fcs:
                fcs.update(dma_free=[ln_done] * 4, cvt_free=[ln_done] * 4,
                           cvt_tok=[None] * 4, ps_free=[None, None], gidx=0)
            emit_fc(p, C, psum, wt_dram=w1t, K=2048, NTILES=32, rhs_fn=rhs1,
                    rhs_tok=xh_tok, Tn=KB, tw=512, evac=evac1, lay=lay_fc, fcs=fcs)
            h1_tok = p.tok("sd")
            if upto == "h1":
                p.wait("sync", h1_tok)
                p.wait("sync", ln_done)
                return
            cross["h1_mm"] = p.tok("td")

            def rhs2(kc, tc):
                return h1h[:, KB * kc + 512 * tc: KB * kc + 512 * (tc + 1)]

            def evac2(j, ps_list, tdtok):
                s = p.e("scalar")
                p.wait("scalar", tdtok)
                if j == 0:
                    p.wait("scalar", cross.get("w3_mm"))
                for tc, ps in enumerate(ps_list):
                    _ = s and s.activation(
                        h2h[:, KB * j + 512 * tc: KB * j + 512 * (tc + 1)], ps,
                        AF.Relu, bias=b2s[:, j:j + 1])
                return ("sd", p.dinc("scalar", "sd"))

            emit_fc(p, C, psum, wt_dram=w2t, K=4096, NTILES=32, rhs_fn=rhs2,
                    rhs_tok=h1_tok, Tn=KB, tw=512, evac=evac2, lay=lay_fc, fcs=fcs)
            h2_tok = p.tok("sd")
            if upto == "h2":
                p.wait("sync", h2_tok)
                p.wait("sync", ln_done)
                return
            cross["h2_mm"] = p.tok("td")

            xi_free = [None, None]
            ost_free = [None, None]

            def rhs3(kc, tc):
                return h2h[:, KB * kc + 512 * tc: KB * kc + 512 * (tc + 1)]

            def evac3(j, ps_list, tdtok):
                b = j % 2
                g2 = p.e("gpsimd")
                p.wait("gpsimd", xi_free[b])
                txi = p.inc(g2 and g2.dma_start(
                    out=xit[b], in_=xi[128 * j:128 * (j + 1), c0:c0 + KB]),
                    "xi%d" % b, 16)
                s = p.e("scalar")
                p.wait("scalar", tdtok)
                p.wait("scalar", ost_free[b])
                for tc, ps in enumerate(ps_list):
                    _ = s and s.activation(
                        ost[b][:, 512 * tc:512 * (tc + 1)], ps, AF.Identity,
                        bias=b3s[:, j:j + 1])
                tsd = p.dinc("scalar", "sd")
                v = p.e("vector")
                p.wait("vector", ("sd", tsd))
                p.wait("vector", ("xi%d" % b, txi))
                _ = v and v.tensor_add(ost[b], ost[b], xit[b])
                tvv = p.dinc("vector", "vd")
                xi_free[b] = ("vd", tvv)
                g3 = p.e("gpsimd")
                p.wait("gpsimd", ("vd", tvv))
                tst = p.inc(g3 and g3.dma_start(
                    out=xo[128 * j:128 * (j + 1), c0:c0 + KB], in_=ost[b]),
                    "os%d" % b, 16)
                ost_free[b] = ("os%d" % b, tst)
                return ("sd", tsd)

            emit_fc(p, C, psum, wt_dram=w3t, K=4096, NTILES=16, rhs_fn=rhs3,
                    rhs_tok=h2_tok, Tn=KB, tw=512, evac=evac3, lay=lay_fc, fcs=fcs)
            cross["w3_mm"] = p.tok("td")

        p.wait("sync", [p.tok("os0"), p.tok("os1")])

    return emit


def build_launch2(upto='full', dbg=False):
    nc = bass.Bass("TRN2", target_bir_lowering=False)
    xi = nc.dram_tensor("xi", [D, T], F32, kind="ExternalInput").ap()
    xo = nc.dram_tensor("xo", [D, T], F32, kind="ExternalOutput").ap()
    return build_engines(nc, make_launch2(nc, xi, xo, "", upto, dbg))


def run_launch2(d, xs, upto='full', dbg=False):
    nc = build_launch2(upto, dbg)
    in_maps = [dict(xi=xs[c], w1t=d["w1t"], b1c=d["b1c"], w2t=d["w2t"], b2c=d["b2c"],
                    w3t=d["w3t"], b3c=d["b3c"]) for c in range(len(xs))]
    res = run_bass_kernel_spmd(nc, in_maps, core_ids=list(range(len(xs))))
    return res.results




def emit_attention(p, C, psum, *, nheads, sc, q_dram, k_dram, v_dram,
                   nrow_d, lrow_d, o_sb, lay, masks, tag,
                   qr_dram=None, krot_sb=None, head_free_tok=None):
    mdiag, m1s = masks
    qp = [C.at(lay["qp"] + i * 4096, 128, 2048, BF16) for i in range(2)]
    kp = [C.at(lay["kp"] + i * 4096, 128, 2048, BF16) for i in range(2)]
    vv_ = [C.at(lay["vv"] + i * 4096, 128, 2048, BF16) for i in range(2)]
    qr = [C.at(lay["qr"] + i * 4096, 64, 2048, BF16) for i in range(2)] \
        if qr_dram is not None else None
    linvB = [C.at(lay["linvB"] + i * 8192, 128, 2048, F32) for i in range(2)]
    negm = [C.at(lay["negm"] + i * 4096, 1, 2048, BF16) for i in range(2)]
    scr = [C.at(lay["scr"] + i * 2048, 128, 512, F32) for i in range(2)]
    pT = [C.at(lay["pT"] + i * 1024, 128, 512, BF16) for i in range(4)]
    dg = [C.at(lay["dg"] + i * 512, 128, 128, F32) for i in range(2)]
    ST = [C.at(lay["ST"] + i * 256, 128, 64, F32) for i in range(2)]
    NM = [C.at(lay["NM"] + i * 32, 128, 16, BF16) for i in range(2)]
    LV = [C.at(lay["LV"] + i * 64, 128, 16, F32) for i in range(2)]
    ones_bf = C.at(lay["ones_bf"], 1, 128, BF16)

    v = p.e("vector")
    ones_tok = ("vd", p.dinc("vector", "vd")) if (v and v.memset(ones_bf, 1.0)) or True \
        else None

    slot_free = [head_free_tok, head_free_tok]
    ps_tok = [None, None, None, None]
    dg_tok = [None, None]
    for h in range(nheads):
        hb = h % 2
        g = p.e("gpsimd")
        p.wait("gpsimd", slot_free[hb])
        tq = p.inc(g and g.dma_start(out=qp[hb],
                                     in_=q_dram[128 * h:128 * (h + 1), :]),
                   tag + "q%d" % hb, 16)
        tk = p.inc(g and g.dma_start(out=kp[hb],
                                     in_=k_dram[128 * h:128 * (h + 1), :]),
                   tag + "k%d" % hb, 16)
        toks = [(tag + "q%d" % hb, tq), (tag + "k%d" % hb, tk)]
        if qr is not None:
            tr = p.inc(g and g.dma_start(out=qr[hb],
                                         in_=qr_dram[64 * h:64 * (h + 1), :]),
                       tag + "r%d" % hb, 16)
            toks.append((tag + "r%d" % hb, tr))
        p.wait("tensor", toks)
        if h == 0:
            p.wait("tensor", ones_tok)
        st_ = ST[hb]
        mrow_c = st_[:, 0:8]; biasc = st_[:, 8:16]
        zra = st_[:, 16:24]; zrb = st_[:, 24:32]
        m_c = st_[:, 32:33]; mp_c = st_[:, 33:34]; z_c = st_[:, 34:35]
        dd = st_[:, 40:48]; fsb = st_[:, 48:56]; zw = st_[:, 56:64]
        for qt in range(16):
            vv2 = p.e("vector")
            p.wait("vector", ps_tok)
            _ = vv2 and vv2.memset(zra, 0.0)
            _ = vv2 and vv2.memset(zrb, 0.0)
            tz = p.dinc("vector", "vd")
            nkc = (128 * (qt + 1) + 511) // 512
            for kc in range(nkc):
                w = min(512, 128 * (qt + 1) - 512 * kc)
                bank = psum[4 + (kc % 4)]
                t = p.e("tensor")
                p.wait("tensor", ps_tok[kc % 4])
                last = t and t.matmul(bank[:, 0:w], qp[hb][:, 128 * qt:128 * (qt + 1)],
                                      kp[hb][:, 512 * kc:512 * kc + w],
                                      start=True, stop=(qr is None))
                if qr is not None:
                    last = t and t.matmul(bank[:, 0:w], qr[hb][:, 128 * qt:128 * (qt + 1)],
                                          krot_sb[:, 512 * kc:512 * kc + w],
                                          start=False, stop=True)
                tdk = p.inc(last, "td", 1)
                vv2 = p.e("vector")
                p.wait("vector", ("td", tdk))
                diag = (kc == nkc - 1)
                if diag:
                    p.wait("vector", dg_tok[qt % 2])
                    _ = vv2 and vv2.scalar_tensor_tensor(dg[qt % 2], bank[:, w - 128:w],
                                                         0.0, mdiag, op0=ALU.add,
                                                         op1=ALU.add)
                _ = vv2 and vv2.reduce_max(mrow_c[:, kc:kc + 1], bank[:, 0:w], axis=AX.X)
                p.fence("vector", "vd")
                _ = vv2 and vv2.tensor_scalar(biasc[:, kc:kc + 1], mrow_c[:, kc:kc + 1],
                                              -sc, None, op0=ALU.mult)
                tvb = p.dinc("vector", "vd")
                s = p.e("scalar")
                p.wait("scalar", ("vd", tvb))
                if qt == 0 and kc == 0:
                    p.wait("scalar", ("vd", tz))
                nd_w = (w - 128) if diag else w
                if nd_w > 0:
                    _ = s and s.activation(scr[kc % 2][:, 0:nd_w], bank[:, 0:nd_w],
                                           AF.Exp, bias=biasc[:, kc:kc + 1], scale=sc,
                                           accum_out=zra[:, kc:kc + 1])
                if diag:
                    _ = s and s.activation(scr[kc % 2][:, 384:512], dg[qt % 2], AF.Exp,
                                           bias=biasc[:, kc:kc + 1], scale=sc,
                                           accum_out=zrb[:, kc:kc + 1])
                tse = p.dinc("scalar", "sd")
                ps_tok[kc % 4] = ("sd", tse)
                if diag:
                    dg_tok[qt % 2] = ("sd", tse)
            vv2 = p.e("vector")
            p.wait("vector", ps_tok[(nkc - 1) % 4])
            _ = vv2 and vv2.reduce_max(m_c, mrow_c[:, 0:nkc], axis=AX.X)
            p.fence("vector", "vd")
            _ = vv2 and vv2.tensor_scalar(NM[hb][:, qt:qt + 1], m_c, -1.0, None,
                                          op0=ALU.mult)
            p.fence("vector", "vd")
            _ = vv2 and vv2.tensor_scalar(mp_c, NM[hb][:, qt:qt + 1], -1.0, None,
                                          op0=ALU.mult)
            p.fence("vector", "vd")
            _ = vv2 and vv2.tensor_scalar(dd[:, 0:nkc], mrow_c[:, 0:nkc], mp_c, sc,
                                          op0=ALU.subtract, op1=ALU.mult)
            tvd = p.dinc("vector", "vd")
            s = p.e("scalar")
            p.wait("scalar", ("vd", tvd))
            _ = s and s.activation(fsb[:, 0:nkc], dd[:, 0:nkc], AF.Exp)
            tsf = p.dinc("scalar", "sd")
            vv2 = p.e("vector")
            p.wait("vector", ("sd", tsf))
            _ = vv2 and vv2.tensor_add(zw[:, 0:nkc], zra[:, 0:nkc], zrb[:, 0:nkc])
            p.fence("vector", "vd")
            _ = vv2 and vv2.tensor_mul(zw[:, 0:nkc], zw[:, 0:nkc], fsb[:, 0:nkc])
            p.fence("vector", "vd")
            _ = vv2 and vv2.reduce_sum(z_c, zw[:, 0:nkc], axis=AX.X)
            p.fence("vector", "vd")
            _ = vv2 and vv2.reciprocal(LV[hb][:, qt:qt + 1], z_c)
            p.fence("vector", "vd")
        thd = p.dinc("vector", "vd")
        sq = p.e("scalar")
        p.wait("scalar", ("vd", thd))
        gsem = tag + "g%d" % hb
        tg = 0
        for qt in range(16):
            tg = p.inc(sq and sq.dma_start(out=nrow_d[h:h + 1, 128 * qt:128 * (qt + 1)],
                                           in_=NM[hb][:, qt:qt + 1]), gsem, 16)
            tg = p.inc(sq and sq.dma_start(out=lrow_d[h:h + 1, 128 * qt:128 * (qt + 1)],
                                           in_=LV[hb][:, qt:qt + 1]), gsem, 16)
        slot_free[hb] = (gsem, tg)
    gather_done = [p.tok(tag + "g0"), p.tok(tag + "g1")]

    slot_free = [gather_done, gather_done]
    obank_free = [None] * 4
    pT_free = [None] * 4
    psT_free = [None] * 4
    for h in range(nheads):
        hb = h % 2
        g = p.e("gpsimd")
        p.wait("gpsimd", slot_free[hb])
        tq = p.inc(g and g.dma_start(out=qp[hb],
                                     in_=q_dram[128 * h:128 * (h + 1), :]),
                   tag + "q%d" % hb, 16)
        tk = p.inc(g and g.dma_start(out=kp[hb],
                                     in_=k_dram[128 * h:128 * (h + 1), :]),
                   tag + "k%d" % hb, 16)
        tv_ = 0
        for kt in range(16):
            tv_ = p.inc(g and g.dma_start(
                out=vv_[hb][:, 128 * kt:128 * (kt + 1)],
                in_=v_dram[128 * kt:128 * (kt + 1), 128 * h:128 * (h + 1)]),
                tag + "v%d" % hb, 16)
        toks = [(tag + "q%d" % hb, tq), (tag + "k%d" % hb, tk),
                (tag + "v%d" % hb, tv_)]
        if qr is not None:
            tr = p.inc(g and g.dma_start(out=qr[hb],
                                         in_=qr_dram[64 * h:64 * (h + 1), :]),
                       tag + "r%d" % hb, 16)
            toks.append((tag + "r%d" % hb, tr))
        sq = p.e("scalar")
        if h < 2:
            p.wait("scalar", gather_done)
        p.wait("scalar", slot_free[hb])
        tn = p.inc(sq and sq.dma_start(out=negm[hb], in_=nrow_d[h:h + 1, :]),
                   tag + "n%d" % hb, 16)
        lb_src = bass.AP(lrow_d.tensor, 2048 * h, [[0, 128], [1, 2048]])
        tl = p.inc(sq and sq.dma_start(out=linvB[hb], in_=lb_src),
                   tag + "l%d" % hb, 16)
        toks.append((tag + "n%d" % hb, tn))
        p.wait("tensor", toks)
        lv_tok = (tag + "l%d" % hb, tl)
        pp = 0
        last_mm_tok = None
        for kt in range(16):
            for qc in range(kt // 4, 4):
                bank = psum[4 + pp]
                t = p.e("tensor")
                p.wait("tensor", psT_free[pp])
                last = t and t.matmul(bank[:, 0:512], kp[hb][:, 128 * kt:128 * (kt + 1)],
                                      qp[hb][:, 512 * qc:512 * (qc + 1)],
                                      start=True, stop=False)
                if qr is not None:
                    last = t and t.matmul(bank[:, 0:512], krot_sb[:, 128 * kt:128 * (kt + 1)],
                                          qr[hb][:, 512 * qc:512 * (qc + 1)],
                                          start=False, stop=False)
                last = t and t.matmul(bank[:, 0:512], ones_bf,
                                      negm[hb][0:1, 512 * qc:512 * (qc + 1)],
                                      start=False, stop=True)
                tdk = p.inc(last, "td", 1)
                s = p.e("scalar")
                p.wait("scalar", ("td", tdk))
                p.wait("scalar", pT_free[pp])
                _ = s and s.activation(pT[pp], bank[:, 0:512], AF.Exp, scale=sc)
                tpe = p.dinc("scalar", "sd")
                psT_free[pp] = ("sd", tpe)
                pt_tok = ("sd", tpe)
                if qc == kt // 4:
                    vv2 = p.e("vector")
                    p.wait("vector", ("sd", tpe))
                    _ = vv2 and vv2.tensor_mul(pT[pp], pT[pp], m1s[kt % 4])
                    tpm = p.dinc("vector", "vd")
                    pt_tok = ("vd", tpm)
                t = p.e("tensor")
                p.wait("tensor", pt_tok)
                if kt == 0:
                    p.wait("tensor", obank_free[qc])
                lmm = t and t.matmul(psum[qc][:, 0:512],
                                     vv_[hb][:, 128 * kt:128 * (kt + 1)], pT[pp],
                                     start=(kt == 0), stop=(kt == 4 * qc + 3))
                tdo = p.inc(lmm, "td", 1)
                pT_free[pp] = ("td", tdo)
                last_mm_tok = ("td", tdo)
                if kt == 4 * qc + 3:
                    vv2 = p.e("vector")
                    p.wait("vector", ("td", tdo))
                    p.wait("vector", lv_tok)
                    _ = vv2 and vv2.tensor_tensor(
                        o_sb[:, 2048 * h + 512 * qc:2048 * h + 512 * (qc + 1)],
                        psum[qc][:, 0:512], linvB[hb][:, 512 * qc:512 * (qc + 1)],
                        op=ALU.mult)
                    teo = p.dinc("vector", "vd")
                    obank_free[qc] = ("vd", teo)
                pp = (pp + 1) % 4
        slot_free[hb] = [last_mm_tok, obank_free[3]]
    return [p.tok("vd"), p.tok("td")]




def make_launch3(nc, xi, xo, pref='', dbg=False):
    wq1 = nc.dram_tensor(pref + "wq1", [2048, 2048], F32, kind="ExternalInput").ap()
    wk1 = nc.dram_tensor(pref + "wk1", [2048, 2048], F32, kind="ExternalInput").ap()
    wv1 = nc.dram_tensor(pref + "wv1", [2048, 2048], F32, kind="ExternalInput").ap()
    wo1 = nc.dram_tensor(pref + "wo1", [2048, 2048], F32, kind="ExternalInput").ap()
    mdiag_d = nc.dram_tensor(pref + "mdiag", [128, 128], F32, kind="ExternalInput").ap()
    m1s_d = nc.dram_tensor(pref + "m1s", [4, 128, 512], BF16, kind="ExternalInput").ap()
    qs = nc.dram_tensor(pref + "qs", [2048, T], BF16, kind="Internal").ap()
    ks = nc.dram_tensor(pref + "ks", [2048, T], BF16, kind="Internal").ap()
    vs = nc.dram_tensor(pref + "vs", [T, 2048], BF16, kind="Internal").ap()
    nrow = nc.dram_tensor(pref + "nrow", [16, T], BF16, kind="Internal").ap()
    lrow = nc.dram_tensor(pref + "lrow", [16, T], F32, kind="Internal").ap()
    rows_d = nc.dram_tensor(pref + "rows_d", [2, 2048], F32, kind="Internal").ap()

    def emit(p, C, psum):
        lay_ln = dict(xt=90112, sq=106496, mB=122880, rB=131072, s16=139264,
                      ones=139776)
        lay_fc = dict(sf=0, sb=16384)
        lay_at = dict(qp=0, kp=8192, vv=16384, linvB=155648, scr=172032,
                      pT=176128, dg=180224, ST=181248, NM=181760, LV=181824,
                      ones_bf=181952, negm=32768, qr=0)
        xln = C.at(24576, 128, 16 * 2048, BF16)
        o_sb = C.at(90112, 128, 16 * 2048, BF16)
        wvt = [C.at(90112 + dc * 4096, 128, 2048, BF16) for dc in range(16)]
        wvf = C.at(0, 128, 2048, F32)
        mdiag = C.at(187840, 128, 128, F32)
        m1s = [C.at(188352 + i * 1024, 128, 512, BF16) for i in range(4)]
        stg = [C.at(192448 + i * 4096, 128, 2048, BF16) for i in range(2)]
        xit = [C.at(155648 + i * 8192, 128, 2048, F32) for i in range(2)]
        ost = [C.at(172032 + i * 8192, 128, 2048, F32) for i in range(2)]

        g = p.e("gpsimd")
        p.inc(g and g.dma_start(out=mdiag, in_=mdiag_d), "msk", 16)
        for i in range(4):
            tm = p.inc(g and g.dma_start(out=m1s[i], in_=m1s_d[i]), "msk", 16)
        mask_tok = ("msk", tm)

        def ln_out(k, xt_ap, mB, rB, tok):
            v = p.e("vector")
            p.wait("vector", tok)
            _ = v and v.scalar_tensor_tensor(xt_ap, xt_ap, 1.0, mB, op0=ALU.mult,
                                             op1=ALU.subtract)
            _ = v and v.tensor_tensor(xln[:, 2048 * k:2048 * (k + 1)], xt_ap, rB,
                                      op=ALU.mult)
            return ("vd", p.dinc("vector", "vd"))

        emit_ln(p, C, psum, x_dram=xi, rows_dram=rows_d, out_cb=ln_out,
                lay=lay_ln, tag="ln")
        ln_done = p.tok("vd")

        fcs = {}
        stg_free = [None, None]

        def store_fc(wt, dst, ssem):
            def rhs(kc, tc):
                return xln[:, 2048 * kc + 512 * tc:2048 * kc + 512 * (tc + 1)]

            def evac(j, ps_list, tdtok):
                b = j % 2
                v = p.e("vector")
                p.wait("vector", tdtok)
                p.wait("vector", stg_free[b])
                for tc, ps in enumerate(ps_list):
                    _ = v and v.tensor_copy(stg[b][:, 512 * tc:512 * (tc + 1)], ps)
                tvv = p.dinc("vector", "vd")
                g2 = p.e("gpsimd")
                p.wait("gpsimd", ("vd", tvv))
                tst = p.inc(g2 and g2.dma_start(
                    out=dst[128 * j:128 * (j + 1), :], in_=stg[b]),
                    ssem + "%d" % b, 16)
                stg_free[b] = (ssem + "%d" % b, tst)
                return ("vd", tvv)

            emit_fc(p, C, psum, wt_dram=wt, K=2048, NTILES=16, rhs_fn=rhs,
                    rhs_tok=ln_done, Tn=2048, tw=512, evac=evac, lay=lay_fc,
                    fcs=fcs)

        store_fc(wq1, qs, "qS")
        store_fc(wk1, ks, "kS")
        qk_done = [p.tok("qS0"), p.tok("qS1"), p.tok("kS0"), p.tok("kS1")]

        qkmm_done = p.tok("td")
        last_cvt = None
        for dc in range(16):
            s_ = p.e("sync")
            if dc == 0:
                p.wait("sync", ln_done)
                p.wait("sync", qkmm_done)
            else:
                p.wait("sync", ("sd", last_cvt))
            tw_ = p.inc(s_ and s_.dma_start(out=wvf, in_=wv1[128 * dc:128 * (dc + 1), :]),
                        "wvl", 16)
            sc_ = p.e("scalar")
            p.wait("scalar", ("wvl", tw_))
            _ = sc_ and sc_.copy(wvt[dc], wvf)
            last_cvt = p.dinc("scalar", "sd")
        wv_tok = ("sd", last_cvt)

        vps_free = [None, None]
        for tt in range(16):
            t = p.e("tensor")
            p.wait("tensor", wv_tok)
            p.wait("tensor", vps_free[tt % 2])
            last = None
            for dc in range(16):
                for vc in range(4):
                    last = t and t.matmul(psum[4 * (tt % 2) + vc][:, 0:512],
                                          xln[:, 2048 * dc + 128 * tt:2048 * dc + 128 * (tt + 1)],
                                          wvt[dc][:, 512 * vc:512 * (vc + 1)],
                                          start=(dc == 0), stop=(dc == 15))
            tdk = p.inc(last, "td", 1)
            b = tt % 2
            v = p.e("vector")
            p.wait("vector", ("td", tdk))
            p.wait("vector", stg_free[b])
            for vc in range(4):
                _ = v and v.tensor_copy(stg[b][:, 512 * vc:512 * (vc + 1)],
                                        psum[4 * b + vc][:, 0:512])
            tvv = p.dinc("vector", "vd")
            vps_free[b] = ("vd", tvv)
            g2 = p.e("gpsimd")
            p.wait("gpsimd", ("vd", tvv))
            tst = p.inc(g2 and g2.dma_start(out=vs[128 * tt:128 * (tt + 1), :],
                                            in_=stg[b]), "vS%d" % b, 16)
            stg_free[b] = ("vS%d" % b, tst)
        v_done = [p.tok("vS0"), p.tok("vS1")]

        head_free = qk_done + v_done + [mask_tok]
        at_done = emit_attention(p, C, psum, nheads=16, sc=1.0 / float(np.sqrt(128.0)),
                                 q_dram=qs, k_dram=ks, v_dram=vs, nrow_d=nrow,
                                 lrow_d=lrow, o_sb=o_sb, lay=lay_at,
                                 masks=(mdiag, m1s), tag="at",
                                 head_free_tok=head_free)

        fcs["dma_free"] = [at_done] * 4
        fcs["cvt_free"] = [at_done] * 4
        fcs["ps_free"] = [at_done, at_done]
        xi_free = [at_done, at_done]
        ost_free = [at_done, at_done]

        def rhs_o(kc, tc):
            return o_sb[:, 2048 * kc + 512 * tc:2048 * kc + 512 * (tc + 1)]

        def evac_o(j, ps_list, tdtok):
            b = j % 2
            g2 = p.e("gpsimd")
            p.wait("gpsimd", xi_free[b])
            txi = p.inc(g2 and g2.dma_start(out=xit[b],
                                            in_=xi[128 * j:128 * (j + 1), :]),
                        "xi%d" % b, 16)
            v = p.e("vector")
            p.wait("vector", tdtok)
            p.wait("vector", ("xi%d" % b, txi))
            p.wait("vector", ost_free[b])
            for tc, ps in enumerate(ps_list):
                _ = v and v.tensor_add(ost[b][:, 512 * tc:512 * (tc + 1)], ps,
                                       xit[b][:, 512 * tc:512 * (tc + 1)])
            tvv = p.dinc("vector", "vd")
            xi_free[b] = ("vd", tvv)
            g3 = p.e("gpsimd")
            p.wait("gpsimd", ("vd", tvv))
            tst = p.inc(g3 and g3.dma_start(out=xo[128 * j:128 * (j + 1), :],
                                            in_=ost[b]), "oS%d" % b, 16)
            ost_free[b] = ("oS%d" % b, tst)
            return ("vd", tvv)

        emit_fc(p, C, psum, wt_dram=wo1, K=2048, NTILES=16, rhs_fn=rhs_o,
                rhs_tok=at_done, Tn=2048, tw=512, evac=evac_o, lay=lay_fc, fcs=fcs)

        p.wait("sync", [p.tok("oS0"), p.tok("oS1")])

    return emit


def build_launch3(dbg=False):
    nc = bass.Bass("TRN2", target_bir_lowering=False)
    xi = nc.dram_tensor("xi", [D, T], F32, kind="ExternalInput").ap()
    xo = nc.dram_tensor("xo", [D, T], F32, kind="ExternalOutput").ap()
    return build_engines(nc, make_launch3(nc, xi, xo, "", dbg))


def run_launch3(d, xs, dbg=False):
    nc = build_launch3(dbg)
    in_maps = [dict(xi=xs[c], wq1=d["wq1"], wk1=d["wk1"], wv1=d["wv1"],
                    wo1=d["wo1"], mdiag=d["mdiag"], m1s=d["m1s"])
               for c in range(len(xs))]
    res = run_bass_kernel_spmd(nc, in_maps, core_ids=list(range(len(xs))))
    return res.results




def make_launch1(nc, xi, xo, pref='', dbg=False, upto='full'):
    wqa = nc.dram_tensor(pref + "wqa", [2048, 1536], F32, kind="ExternalInput").ap()
    qnw = nc.dram_tensor(pref + "qnw", [128, 12], F32, kind="ExternalInput").ap()
    wqb = nc.dram_tensor(pref + "wqb", [1536, 3072], F32, kind="ExternalInput").ap()
    wkr = nc.dram_tensor(pref + "wkr", [2048, 64], F32, kind="ExternalInput").ap()
    wkva = nc.dram_tensor(pref + "wkva", [2048, 512], F32, kind="ExternalInput").ap()
    kvnw = nc.dram_tensor(pref + "kvnw", [128, 4], F32, kind="ExternalInput").ap()
    wkvb = nc.dram_tensor(pref + "wkvb", [512, 4096], F32, kind="ExternalInput").ap()
    wo0 = nc.dram_tensor(pref + "wo0", [2048, 2048], F32, kind="ExternalInput").ap()
    cos4_d = nc.dram_tensor(pref + "cos4", [128, T], BF16, kind="ExternalInput").ap()
    sin4_d = nc.dram_tensor(pref + "sin4", [128, T], BF16, kind="ExternalInput").ap()
    mdiag_d = nc.dram_tensor(pref + "mdiag", [128, 128], F32, kind="ExternalInput").ap()
    m1s_d = nc.dram_tensor(pref + "m1s", [4, 128, 512], BF16, kind="ExternalInput").ap()
    qs = nc.dram_tensor(pref + "qs", [2048, T], BF16, kind="Internal").ap()
    qrs = nc.dram_tensor(pref + "qrs", [1024, T], BF16, kind="Internal").ap()
    ks = nc.dram_tensor(pref + "ks", [2048, T], BF16, kind="Internal").ap()
    vs = nc.dram_tensor(pref + "vs", [T, 2048], BF16, kind="Internal").ap()
    nrow = nc.dram_tensor(pref + "nrow", [16, T], BF16, kind="Internal").ap()
    lrow = nc.dram_tensor(pref + "lrow", [16, T], F32, kind="Internal").ap()
    rows_d = nc.dram_tensor(pref + "rows_d", [2, 2048], F32, kind="Internal").ap()
    qrow_d = nc.dram_tensor(pref + "qrow_d", [2, 2048], F32, kind="Internal").ap()

    def emit(p, C, psum):
        lay_ln = dict(xt=90112, sq=106496, mB=122880, rB=131072, s16=178688,
                      ones=179072)
        lay_fc = dict(sf=0, sb=16384)
        lay_at = dict(qp=0, kp=8192, vv=16384, qr=24576, negm=32768,
                      linvB=155648, scr=172032, pT=176128, dg=180224, ST=181248,
                      NM=181760, LV=181824, ones_bf=181952)
        xln = C.at(24576, 128, 16 * 2048, BF16)
        o_sb = C.at(90112, 128, 16 * 2048, BF16)
        qa_bf = C.at(90112, 128, 12 * 2048, BF16)
        kn = C.at(139264, 128, 4 * 2048, BF16)
        rstage = [C.at(155648 + i * 4096, 128, 2048, BF16) for i in range(2)]
        sqs = C.at(163840, 128, 2048, F32)
        sqb = C.at(182272, 128, 2048, BF16)
        rsc = C.at(172032, 128, 512, F32)
        rstdB = C.at(174080, 128, 2048, F32)
        stg = [C.at(182272 + i * 4096, 128, 2048, BF16) for i in range(2)]
        xit = [C.at(155648 + i * 8192, 128, 2048, F32) for i in range(2)]
        ost = [C.at(172032 + i * 8192, 128, 2048, F32) for i in range(2)]
        cos4 = C.at(194816, 128, 2048, BF16)
        sin4 = C.at(198912, 128, 2048, BF16)
        m1s = [C.at(203008 + i * 1024, 128, 512, BF16) for i in range(4)]
        mdiag = C.at(207104, 128, 128, F32)
        krot_sb = C.at(207616, 64, 2048, BF16)

        g = p.e("gpsimd")
        p.inc(g and g.dma_start(out=mdiag, in_=mdiag_d), "msk", 16)
        p.inc(g and g.dma_start(out=cos4, in_=cos4_d), "msk", 16)
        p.inc(g and g.dma_start(out=sin4, in_=sin4_d), "msk", 16)
        for i in range(4):
            tm = p.inc(g and g.dma_start(out=m1s[i], in_=m1s_d[i]), "msk", 16)
        mask_tok = ("msk", tm)
        qnw_s = C.at(211712, 128, 12, F32)
        kvnw_s = C.at(211776, 128, 4, F32)
        p.inc(g and g.dma_start(out=qnw_s, in_=qnw), "msk", 16)
        tm2 = p.inc(g and g.dma_start(out=kvnw_s, in_=kvnw), "msk", 16)
        nw_tok = ("msk", tm2)

        def ln_out(k, xt_ap, mB, rB, tok):
            v = p.e("vector")
            p.wait("vector", tok)
            _ = v and v.scalar_tensor_tensor(xt_ap, xt_ap, 1.0, mB, op0=ALU.mult,
                                             op1=ALU.subtract)
            _ = v and v.tensor_tensor(xln[:, 2048 * k:2048 * (k + 1)], xt_ap, rB,
                                      op=ALU.mult)
            return ("vd", p.dinc("vector", "vd"))

        emit_ln(p, C, psum, x_dram=xi, rows_dram=rows_d, out_cb=ln_out,
                lay=lay_ln, tag="ln")
        ln_done = p.tok("vd")
        if upto == "ln":
            p.wait("sync", ln_done)
            return

        fcs = {}

        def lowrank(wt, K_, ntiles, dst_bf, stat_bank, row_gate=None):
            for hf in range(2):
                def rhs(kc, tc, hf=hf):
                    return xln[:, 2048 * kc + 1024 * hf + 512 * tc:
                               2048 * kc + 1024 * hf + 512 * (tc + 1)]

                def evac(j, ps_list, tdtok, hf=hf):
                    s = p.e("scalar")
                    p.wait("scalar", tdtok)
                    for tc, ps in enumerate(ps_list):
                        _ = s and s.activation(sqb[:, 1024 * hf + 512 * tc:
                                                   1024 * hf + 512 * (tc + 1)], ps,
                                               AF.Square)
                    tsq = p.dinc("scalar", "sd")
                    v = p.e("vector")
                    p.wait("vector", tdtok)
                    p.wait("vector", ("sd", tsq))
                    for tc, ps in enumerate(ps_list):
                        _ = v and v.tensor_copy(
                            dst_bf[:, 2048 * j + 1024 * hf + 512 * tc:
                                   2048 * j + 1024 * hf + 512 * (tc + 1)], ps)
                    tcv = p.dinc("vector", "vd")
                    t = p.e("tensor")
                    p.wait("tensor", ("sd", tsq))
                    if j == 0 and hf == 0:
                        p.wait("tensor", row_gate)
                    last = None
                    for ts8 in range(8):
                        ts = 8 * hf + ts8
                        last = t and t.matmul(
                            psum[stat_bank][:, ntiles * ts + j:ntiles * ts + j + 1],
                            sqb[:, 1024 * hf + 128 * ts8:1024 * hf + 128 * (ts8 + 1)],
                            ones_rms, start=True, stop=True, skip_group_check=True)
                    trm = p.inc(last, "td", 1)
                    return [("vd", tcv), ("td", trm)]

                emit_fc(p, C, psum, wt_dram=wt, K=K_, NTILES=ntiles, rhs_fn=rhs,
                        rhs_tok=ln_done, Tn=1024, tw=512, evac=evac, lay=lay_fc,
                        fcs=fcs)

        ones_rms = C.at(211840, 128, 1, BF16)
        v = p.e("vector")
        _ = v and v.memset(ones_rms, 1.0)
        p.dinc("vector", "vd")

        lowrank(wqa, 2048, 12, qa_bf, 4)
        qa_mm = p.tok("td")
        if upto == "qa":
            p.wait("sync", qa_mm)
            return

        rstat = C.at(211968 - 256 if False else 179200, 128, 48, F32)

        def rms_rows(stat_bank, ntiles, nfeat, row_slot, gate):
            vv2 = p.e("vector")
            p.wait("vector", gate)
            r16 = rstat[:, 0:16]
            _ = vv2 and vv2.reduce_sum(
                r16, psum[stat_bank][:, 0:16 * ntiles].rearrange(
                    "p (a b) -> p a b", b=ntiles), axis=AX.X)
            p.fence("vector", "vd")
            _ = vv2 and vv2.tensor_scalar(r16, r16, 1.0 / nfeat, 1e-6,
                                          op0=ALU.mult, op1=ALU.add)
            tv = p.dinc("vector", "vd")
            s = p.e("scalar")
            p.wait("scalar", ("vd", tv))
            _ = s and s.activation(r16, r16, AF.Sqrt)
            tsv = p.dinc("scalar", "sd")
            p.wait("vector", ("sd", tsv))
            _ = vv2 and vv2.reciprocal(r16, r16)
            trc = p.dinc("vector", "vd")
            sq_ = p.e("scalar")
            p.wait("scalar", ("vd", trc))
            gsem = "qg%d" % row_slot
            tg = 0
            for ts in range(16):
                tg = p.inc(sq_ and sq_.dma_start(
                    out=qrow_d[row_slot:row_slot + 1, 128 * ts:128 * (ts + 1)],
                    in_=r16[:, ts:ts + 1]), gsem, 16)
            g2 = p.e("gpsimd")
            p.wait("gpsimd", (gsem, tg))
            bsrc = bass.AP(qrow_d.tensor, 2048 * row_slot, [[0, 128], [1, 2048]])
            tbb = p.inc(g2 and g2.dma_start(out=rstdB, in_=bsrc), "qb%d" % row_slot, 16)
            return ("qb%d" % row_slot, tbb), ("vd", p.cnt["vd"])

        rb_tok, rows_read_tok = rms_rows(4, 12, 1536, 0, qa_mm)
        vv = p.e("vector")
        p.wait("vector", rb_tok)
        p.wait("vector", nw_tok)
        for j in range(12):
            _ = vv and vv.tensor_tensor(qa_bf[:, 2048 * j:2048 * (j + 1)],
                                        qa_bf[:, 2048 * j:2048 * (j + 1)], rstdB,
                                        op=ALU.mult)
            _ = vv and vv.tensor_scalar(qa_bf[:, 2048 * j:2048 * (j + 1)],
                                        qa_bf[:, 2048 * j:2048 * (j + 1)],
                                        qnw_s[:, j:j + 1], None, op0=ALU.mult)
        qn_tok = ("vd", p.dinc("vector", "vd"))
        if upto == "rms":
            p.wait("sync", qn_tok)
            return

        lowrank(wkva, 2048, 4, kn, 5, row_gate=rows_read_tok)
        kva_mm = p.tok("td")
        rb_tok2, _ = rms_rows(5, 4, 512, 1, kva_mm)
        p.wait("vector", rb_tok2)
        for j in range(4):
            _ = vv and vv.tensor_tensor(kn[:, 2048 * j:2048 * (j + 1)],
                                        kn[:, 2048 * j:2048 * (j + 1)], rstdB,
                                        op=ALU.mult)
            _ = vv and vv.tensor_scalar(kn[:, 2048 * j:2048 * (j + 1)],
                                        kn[:, 2048 * j:2048 * (j + 1)],
                                        kvnw_s[:, j:j + 1], None, op0=ALU.mult)
        kn_tok = ("vd", p.dinc("vector", "vd"))

        kr_f = C.at(0, 128, 1024, F32)
        kr_b = C.at(16384, 128, 1024, BF16)
        s_ = p.e("sync")
        p.wait("sync", p.tok("td"))
        src_ap = bass.AP(wkr.tensor, 0, [[64, 128], [64 * 128, 16], [1, 64]])
        tkr = p.inc(s_ and s_.dma_start(out=kr_f, in_=src_ap), "krl", 16)
        sc_ = p.e("scalar")
        p.wait("scalar", ("krl", tkr))
        _ = sc_ and sc_.copy(kr_b, kr_f)
        tkc = p.dinc("scalar", "sd")
        t = p.e("tensor")
        p.wait("tensor", ("sd", tkc))
        p.wait("tensor", p.tok("vd"))
        last = None
        for tc in range(4):
            for kc in range(16):
                last = t and t.matmul(psum[tc][0:64, 0:512], kr_b[:, 64 * kc:64 * (kc + 1)],
                                      xln[:, 2048 * kc + 512 * tc:2048 * kc + 512 * (tc + 1)],
                                      start=(kc == 0), stop=(kc == 15))
        tkm = p.inc(last, "td", 1)
        kr2 = C.at(174080, 32, 512, BF16)
        vv = p.e("vector")
        g2 = p.e("gpsimd")
        p.wait("vector", ("td", tkm))
        p.wait("vector", mask_tok)
        tdm = 0
        for tc in range(4):
            ps_ = psum[tc]
            sl = slice(512 * tc, 512 * (tc + 1))
            if tc > 0:
                p.wait("vector", ("krx", tdm))
            _ = vv and vv.tensor_copy(sqs[0:32, 0:512], ps_[0:32, 0:512])
            _ = vv and vv.tensor_copy(sqs[32:64, 512:1024], ps_[32:64, 0:512])
            tkv = p.dinc("vector", "vd")
            p.wait("gpsimd", ("vd", tkv))
            tdm = p.inc(g2 and g2.dma_start(out=sqs[0:32, 512:1024],
                                            in_=sqs[32:64, 512:1024]), "krx", 16)
            p.wait("vector", ("krx", tdm))
            _ = vv and vv.tensor_tensor(sqs[0:32, 1024:1536], sqs[0:32, 0:512],
                                        cos4[0:32, sl], op=ALU.mult)
            _ = vv and vv.tensor_tensor(sqs[0:32, 1536:2048], sqs[0:32, 512:1024],
                                        sin4[0:32, sl], op=ALU.mult)
            _ = vv and vv.tensor_sub(krot_sb[0:32, sl], sqs[0:32, 1024:1536],
                                     sqs[0:32, 1536:2048])
            _ = vv and vv.tensor_tensor(sqs[0:32, 1024:1536], sqs[0:32, 512:1024],
                                        cos4[0:32, sl], op=ALU.mult)
            _ = vv and vv.tensor_tensor(sqs[0:32, 1536:2048], sqs[0:32, 0:512],
                                        sin4[0:32, sl], op=ALU.mult)
            _ = vv and vv.tensor_add(kr2[0:32, 0:512], sqs[0:32, 1024:1536],
                                     sqs[0:32, 1536:2048])
            tko = p.dinc("vector", "vd")
            p.wait("gpsimd", ("vd", tko))
            tdm = p.inc(g2 and g2.dma_start(out=krot_sb[32:64, sl], in_=kr2[0:32, 0:512]),
                        "krx", 16)
        p.wait("vector", ("krx", tdm))
        krot_tok = ("vd", p.dinc("vector", "vd"))
        if upto == "krot":
            p.wait("sync", krot_tok)
            return
        fcs["ps_free"] = [krot_tok, krot_tok]
        fcs["dma_free"][0] = [fcs["dma_free"][0], ("sd", tkc)]
        fcs["cvt_free"][0] = [fcs["cvt_free"][0], ("td", tkm)]

        stg_free = [None, None]

        def rhs_q(kc, tc):
            return qa_bf[:, 2048 * kc + 512 * tc:2048 * kc + 512 * (tc + 1)]

        def evac_qp(j, ps_list, tdtok):
            b = j % 2
            v2 = p.e("vector")
            p.wait("vector", tdtok)
            p.wait("vector", stg_free[b])
            for tc, ps in enumerate(ps_list):
                _ = v2 and v2.tensor_copy(stg[b][:, 512 * tc:512 * (tc + 1)], ps)
            tvv = p.dinc("vector", "vd")
            g2 = p.e("gpsimd")
            p.wait("gpsimd", ("vd", tvv))
            tst = p.inc(g2 and g2.dma_start(out=qs[128 * j:128 * (j + 1), :],
                                            in_=stg[b]), "sS%d" % b, 16)
            stg_free[b] = ("sS%d" % b, tst)
            return ("vd", tvv)

        emit_fc(p, C, psum, wt_dram=wqb, K=1536, NTILES=16, rhs_fn=rhs_q,
                rhs_tok=qn_tok, Tn=2048, tw=512, evac=evac_qp, lay=lay_fc, fcs=fcs)

        rot_free = None
        for gp in range(4):
            for half in range(2):
                jj = 16 + 2 * gp + half
                s_ = p.e("sync")
                sl_f = C.at(0 + half * 6144, 128, 1536, F32)
                sl_b = C.at(16384 + half * 3072, 128, 1536, BF16)
                if gp == 0 and half == 0:
                    p.wait("sync", p.tok("sd"))
                else:
                    p.wait("sync", rot_free)
                src_ap = bass.AP(wqb.tensor, 128 * jj,
                                 [[3072, 128], [128 * 3072, 12], [1, 128]])
                twk = p.inc(s_ and s_.dma_start(out=sl_f, in_=src_ap),
                            "rt%d" % half, 16)
                sc2 = p.e("scalar")
                p.wait("scalar", ("rt%d" % half, twk))
                if gp == 0 and half == 0:
                    p.wait("scalar", p.tok("td"))
                _ = sc2 and sc2.copy(sl_b, sl_f)
                tck = p.dinc("scalar", "sd")
                t = p.e("tensor")
                p.wait("tensor", ("sd", tck))
                if gp == 0 and half == 0:
                    p.wait("tensor", fcs["ps_free"][0])
                    p.wait("tensor", fcs["ps_free"][1])
                    p.wait("tensor", rot_free)
                last = None
                for kc in range(12):
                    for tc in range(4):
                        last = t and t.matmul(
                            psum[4 * half + tc][:, 0:512], sl_b[:, 128 * kc:128 * (kc + 1)],
                            qa_bf[:, 2048 * kc + 512 * tc:2048 * kc + 512 * (tc + 1)],
                            start=(kc == 0), stop=(kc == 11))
                tdk = p.inc(last, "td", 1)
            v2 = p.e("vector")
            p.wait("vector", ("td", tdk))
            p.wait("vector", rot_free)
            for tc in range(4):
                sl = slice(512 * tc, 512 * (tc + 1))
                p1 = psum[tc][:, 0:512]; p2 = psum[4 + tc][:, 0:512]
                _ = v2 and v2.tensor_tensor(rsc[:, 0:512], p1, cos4[:, sl], op=ALU.mult)
                _ = v2 and v2.tensor_tensor(sqs[:, 0:512], p2, sin4[:, sl], op=ALU.mult)
                _ = v2 and v2.tensor_sub(rstage[0][:, sl], rsc[:, 0:512], sqs[:, 0:512])
                _ = v2 and v2.tensor_tensor(rsc[:, 0:512], p2, cos4[:, sl], op=ALU.mult)
                _ = v2 and v2.tensor_tensor(sqs[:, 0:512], p1, sin4[:, sl], op=ALU.mult)
                _ = v2 and v2.tensor_add(rstage[1][:, sl], rsc[:, 0:512], sqs[:, 0:512])
            trp = p.dinc("vector", "vd")
            g2 = p.e("gpsimd")
            p.wait("gpsimd", ("vd", trp))
            tqr = 0
            for hh in range(4):
                h = 4 * gp + hh
                tqr = p.inc(g2 and g2.dma_start(
                    out=qrs[64 * h:64 * h + 32, :], in_=rstage[0][32 * hh:32 * (hh + 1), :]),
                    "qrS", 16)
                tqr = p.inc(g2 and g2.dma_start(
                    out=qrs[64 * h + 32:64 * h + 64, :], in_=rstage[1][32 * hh:32 * (hh + 1), :]),
                    "qrS", 16)
            rot_free = ("qrS", tqr)

        def rhs_k(kc, tc):
            return kn[:, 2048 * kc + 512 * tc:2048 * kc + 512 * (tc + 1)]

        def evac_kp(j, ps_list, tdtok):
            b = j % 2
            v2 = p.e("vector")
            p.wait("vector", tdtok)
            p.wait("vector", stg_free[b])
            for tc, ps in enumerate(ps_list):
                _ = v2 and v2.tensor_copy(stg[b][:, 512 * tc:512 * (tc + 1)], ps)
            tvv = p.dinc("vector", "vd")
            g2 = p.e("gpsimd")
            p.wait("gpsimd", ("vd", tvv))
            tst = p.inc(g2 and g2.dma_start(out=ks[128 * j:128 * (j + 1), :],
                                            in_=stg[b]), "sS%d" % b, 16)
            stg_free[b] = ("sS%d" % b, tst)
            return ("vd", tvv)

        fcs["dma_free"] = [rot_free] * 4
        fcs["cvt_free"] = [rot_free] * 4
        fcs["ps_free"] = [rot_free, rot_free]
        emit_fc(p, C, psum, wt_dram=wkvb, K=512, NTILES=16, rhs_fn=rhs_k,
                rhs_tok=kn_tok, Tn=2048, tw=512, evac=evac_kp, lay=lay_fc, fcs=fcs)

        wvt_v = [C.at(0 + dc * 4096, 128, 2048, BF16) for dc in range(4)]
        wvf2 = C.at(16384, 128, 2048, F32)
        kpmm_done = p.tok("td")
        last_cvt = None
        for dc in range(4):
            s2 = p.e("sync")
            if dc == 0:
                p.wait("sync", kpmm_done)
                p.wait("sync", fcs["dma_free"][0])
                p.wait("sync", fcs["dma_free"][1])
                p.wait("sync", fcs["dma_free"][2])
                p.wait("sync", fcs["dma_free"][3])
            else:
                p.wait("sync", ("sd", last_cvt))
            tw2 = p.inc(s2 and s2.dma_start(out=wvf2,
                                            in_=wkvb[128 * dc:128 * (dc + 1), 2048:4096]),
                        "wvl", 16)
            sc2 = p.e("scalar")
            p.wait("scalar", ("wvl", tw2))
            if dc == 0:
                p.wait("scalar", fcs["cvt_free"][0])
                p.wait("scalar", fcs["cvt_free"][1])
            _ = sc2 and sc2.copy(wvt_v[dc], wvf2)
            last_cvt = p.dinc("scalar", "sd")
        wv_tok = ("sd", last_cvt)

        vps_free = [fcs["ps_free"][0], fcs["ps_free"][1]]
        for tt in range(16):
            t = p.e("tensor")
            p.wait("tensor", wv_tok)
            p.wait("tensor", vps_free[tt % 2])
            last = None
            for dc in range(4):
                for vc in range(4):
                    last = t and t.matmul(psum[4 * (tt % 2) + vc][:, 0:512],
                                          kn[:, 2048 * dc + 128 * tt:2048 * dc + 128 * (tt + 1)],
                                          wvt_v[dc][:, 512 * vc:512 * (vc + 1)],
                                          start=(dc == 0), stop=(dc == 3))
            tdk = p.inc(last, "td", 1)
            b = tt % 2
            v2 = p.e("vector")
            p.wait("vector", ("td", tdk))
            p.wait("vector", stg_free[b])
            for vc in range(4):
                _ = v2 and v2.tensor_copy(stg[b][:, 512 * vc:512 * (vc + 1)],
                                          psum[4 * b + vc][:, 0:512])
            tvv = p.dinc("vector", "vd")
            vps_free[b] = ("vd", tvv)
            g2 = p.e("gpsimd")
            p.wait("gpsimd", ("vd", tvv))
            tst = p.inc(g2 and g2.dma_start(out=vs[128 * tt:128 * (tt + 1), :],
                                            in_=stg[b]), "vS%d" % b, 16)
            stg_free[b] = ("vS%d" % b, tst)
        v_done = [p.tok("vS0"), p.tok("vS1"), p.tok("sS0"), p.tok("sS1"),
                  p.tok("qrS")]
        if upto == "proj":
            p.wait("sync", v_done)
            return

        head_free = v_done + [mask_tok, krot_tok]
        at_done = emit_attention(p, C, psum, nheads=16,
                                 sc=1.0 / float(np.sqrt(192.0)),
                                 q_dram=qs, k_dram=ks, v_dram=vs, nrow_d=nrow,
                                 lrow_d=lrow, o_sb=o_sb, lay=lay_at,
                                 masks=(mdiag, m1s), tag="at",
                                 qr_dram=qrs, krot_sb=krot_sb,
                                 head_free_tok=head_free)

        fcs["dma_free"] = [at_done] * 4
        fcs["cvt_free"] = [at_done] * 4
        fcs["ps_free"] = [at_done, at_done]
        xi_free = [at_done, at_done]
        ost_free = [at_done, at_done]

        def rhs_o(kc, tc):
            return o_sb[:, 2048 * kc + 512 * tc:2048 * kc + 512 * (tc + 1)]

        def evac_o(j, ps_list, tdtok):
            b = j % 2
            g2 = p.e("gpsimd")
            p.wait("gpsimd", xi_free[b])
            txi = p.inc(g2 and g2.dma_start(out=xit[b],
                                            in_=xi[128 * j:128 * (j + 1), :]),
                        "xi%d" % b, 16)
            v2 = p.e("vector")
            p.wait("vector", tdtok)
            p.wait("vector", ("xi%d" % b, txi))
            p.wait("vector", ost_free[b])
            for tc, ps in enumerate(ps_list):
                _ = v2 and v2.tensor_add(ost[b][:, 512 * tc:512 * (tc + 1)], ps,
                                         xit[b][:, 512 * tc:512 * (tc + 1)])
            tvv = p.dinc("vector", "vd")
            xi_free[b] = ("vd", tvv)
            g3 = p.e("gpsimd")
            p.wait("gpsimd", ("vd", tvv))
            tst = p.inc(g3 and g3.dma_start(out=xo[128 * j:128 * (j + 1), :],
                                            in_=ost[b]), "oS%d" % b, 16)
            ost_free[b] = ("oS%d" % b, tst)
            return ("vd", tvv)

        emit_fc(p, C, psum, wt_dram=wo0, K=2048, NTILES=16, rhs_fn=rhs_o,
                rhs_tok=at_done, Tn=2048, tw=512, evac=evac_o, lay=lay_fc, fcs=fcs)

        p.wait("sync", [p.tok("oS0"), p.tok("oS1")])

    return emit


def build_launch1(dbg=False, upto='full'):
    nc = bass.Bass("TRN2", target_bir_lowering=False)
    xi = nc.dram_tensor("xi", [D, T], F32, kind="ExternalInput").ap()
    xo = nc.dram_tensor("xo", [D, T], F32, kind="ExternalOutput").ap()
    return build_engines(nc, make_launch1(nc, xi, xo, "", dbg, upto))


def run_launch1(d, xs, dbg=False, upto='full'):
    nc = build_launch1(dbg, upto)
    in_maps = [dict(xi=xs[c], wqa=d["wqa"], qnw=d["qnw"], wqb=d["wqb"],
                    wkr=d["wkr"], wkva=d["wkva"], kvnw=d["kvnw"], wkvb=d["wkvb"],
                    wo0=d["wo0"], cos4=d["cos4b"], sin4=d["sin4b"],
                    mdiag=d["mdiag"], m1s=d["m1s"]) for c in range(len(xs))]
    res = run_bass_kernel_spmd(nc, in_maps, core_ids=list(range(len(xs))))
    return res.results




def make_launch4(nc, xi, xo, pref='', dbg=False):
    gwp = nc.dram_tensor(pref + "gwp", [1024, 16384], I32, kind="ExternalInput").ap()
    gsc = nc.dram_tensor(pref + "gsc", [64, 16384], F32, kind="ExternalInput").ap()
    uwp = nc.dram_tensor(pref + "uwp", [1024, 16384], I32, kind="ExternalInput").ap()
    usc = nc.dram_tensor(pref + "usc", [64, 16384], F32, kind="ExternalInput").ap()
    dwp = nc.dram_tensor(pref + "dwp", [8192, 2048], I32, kind="ExternalInput").ap()
    dsc = nc.dram_tensor(pref + "dsc", [512, 2048], F32, kind="ExternalInput").ap()
    xln_st = nc.dram_tensor(pref + "xln_st", [D, T], BF16, kind="Internal").ap()
    wgd = nc.dram_tensor(pref + "wgd", [2048, 16384], BF16, kind="Internal").ap()
    wud = nc.dram_tensor(pref + "wud", [2048, 16384], BF16, kind="Internal").ap()
    wdd = nc.dram_tensor(pref + "wdd", [16384, 2048], BF16, kind="Internal").ap()
    rows_d = nc.dram_tensor(pref + "rows_d", [2, 2048], F32, kind="Internal").ap()

    KB = 1024

    def emit(p, C, psum):
        lay_ln = dict(xt=90112, sq=106496, mB=122880, rB=131072, s16=139264,
                      ones=139648)
        xlns = [C.at(139776 + k * 4096, 128, 2048, BF16) for k in range(2)]
        stage_free = [None, None]

        def ln_out(k, xt_ap, mB, rB, tok):
            b = k % 2
            v = p.e("vector")
            p.wait("vector", tok)
            p.wait("vector", stage_free[b])
            _ = v and v.scalar_tensor_tensor(xt_ap, xt_ap, 1.0, mB, op0=ALU.mult,
                                             op1=ALU.subtract)
            _ = v and v.tensor_tensor(xlns[b], xt_ap, rB, op=ALU.mult)
            tvv = p.dinc("vector", "vd")
            g2 = p.e("gpsimd")
            p.wait("gpsimd", ("vd", tvv))
            tst = p.inc(g2 and g2.dma_start(out=xln_st[128 * k:128 * (k + 1), :],
                                            in_=xlns[b]), "xs%d" % b, 16)
            stage_free[b] = ("xs%d" % b, tst)
            return ("vd", tvv)

        emit_ln(p, C, psum, x_dram=xi, rows_dram=rows_d, out_cb=ln_out,
                lay=lay_ln, tag="ln")
        ln_done = [p.tok("xs0"), p.tok("xs1")]

        pk = [C.at(i * 8192, 128, 2048, I32) for i in range(2)]
        scs = [C.at(86016 + i * 8192, 128, 2048, F32) for i in range(2)]
        cl = C.at(16384, 128, 2048, I32)
        ch = C.at(24576, 128, 2048, I32)
        a_ = C.at(32768, 128, 2048, I32)
        b_ = C.at(40960, 128, 2048, I32)
        u_ = C.at(49152, 128, 2048, I32)
        w_ = C.at(57344, 128, 2048, I32)
        bits = C.at(65536, 128, 2048, U16)
        wlo = C.at(69632, 128, 2048, BF16)
        whi = C.at(73728, 128, 2048, BF16)

        def dq_nib(v, c_ap, out_ap, sc_ap):
            _ = v and v.tensor_scalar(a_, c_ap, 8, 12, op0=ALU.bitwise_and,
                                      op1=ALU.logical_shift_left)
            _ = v and v.tensor_scalar(b_, c_ap, 7, 6, op0=ALU.bitwise_and,
                                      op1=ALU.logical_shift_left)
            _ = v and v.scalar_tensor_tensor(u_, b_, 252, a_, op0=ALU.mult, op1=ALU.add)
            _ = v and v.scalar_tensor_tensor(w_, b_, 16128, a_, op0=ALU.add, op1=ALU.add)
            _ = v and v.tensor_tensor(bits, u_, w_, op=ALU.min)
            _ = v and v.tensor_tensor(out_ap, bits.bitcast(BF16), sc_ap, op=ALU.mult)

        dqs = dict(pk=[ln_done, ln_done], sc=[ln_done, ln_done], out=[None, None])

        def dq_matrix(wp, st, wout, K2, N, tag):
            nslab = K2 // 128
            nch = N // 2048
            pk_tok = dqs["pk"]
            sc_tok = dqs["sc"]
            out_tok = dqs["out"]
            dqv = 0
            for s in range(nslab):
                for c in range(nch):
                    b = (s * nch + c) % 2
                    n0 = 2048 * c
                    sy = p.e("sync")
                    p.wait("sync", pk_tok[b])
                    src = bass.AP(wp.tensor, 128 * s * N + n0, [[N, 128], [1, 2048]])
                    tpk = p.inc(sy and sy.dma_start(out=pk[b], in_=src),
                                tag + "p%d" % b, 16)
                    p.wait("sync", sc_tok[b])
                    ssrc = bass.AP(st.tensor, 8 * s * N + n0,
                                   [[N, 8], [0, 16], [1, 2048]])
                    tsc = p.inc(sy and sy.dma_start(out=scs[b], in_=ssrc),
                                tag + "s%d" % b, 16)
                    v = p.e("vector")
                    p.wait("vector", (tag + "p%d" % b, tpk))
                    p.wait("vector", (tag + "s%d" % b, tsc))
                    p.wait("vector", out_tok[0])
                    p.wait("vector", out_tok[1])
                    _ = v and v.tensor_scalar(cl, pk[b], 15, None, op0=ALU.bitwise_and)
                    _ = v and v.tensor_scalar(ch, pk[b], 4, None,
                                              op0=ALU.logical_shift_right)
                    pk_tok[b] = ("vd", p.dinc("vector", "vd"))
                    dq_nib(v, cl, wlo, scs[b])
                    dq_nib(v, ch, whi, scs[b])
                    dqv = p.dinc("vector", "vd")
                    sc_tok[b] = ("vd", dqv)
                    g2 = p.e("gpsimd")
                    p.wait("gpsimd", ("vd", dqv))
                    dlo = bass.AP(wout.tensor, (256 * s) * N + n0, [[2 * N, 128], [1, 2048]])
                    dhi = bass.AP(wout.tensor, (256 * s + 1) * N + n0,
                                  [[2 * N, 128], [1, 2048]])
                    p.inc(g2 and g2.dma_start(out=dlo, in_=wlo), tag + "ol", 16)
                    tqh = p.inc(g2 and g2.dma_start(out=dhi, in_=whi), tag + "oh", 16)
                    out_tok[0] = (tag + "ol", p.cnt[tag + "ol"])
                    out_tok[1] = (tag + "oh", tqh)
            return [p.tok(tag + "ol"), p.tok(tag + "oh")]

        dq1 = dq_matrix(gwp, gsc, wgd, 1024, 16384, "dq")
        dq2 = dq_matrix(uwp, usc, wud, 1024, 16384, "dq")
        dq3 = dq_matrix(dwp, dsc, wdd, 8192, 2048, "dq")
        dq_done = dq1 + dq2 + dq3

        xh = C.at(24576, 128, 16 * KB, BF16)
        wds = [C.at(57344 + i * 4096, 128, 2048, BF16) for i in range(8)]
        inter = C.at(155648, 128, 8 * KB, BF16)
        g1s = C.at(0, 128, KB, F32)
        g2s = C.at(4096, 128, KB, F32)
        acc = C.at(90112, 128, 16 * KB, F32)
        xit = [C.at(188416 + i * 4096, 128, KB, F32) for i in range(2)]
        ost = [C.at(196608 + i * 4096, 128, KB, F32) for i in range(2)]
        gsl = [C.at(172032 + i * 4096, 128, 2048, BF16) for i in range(2)]
        usl = [C.at(180224 + i * 4096, 128, 2048, BF16) for i in range(2)]

        NSL = 2
        gsl_tok = [dq_done] * NSL
        usl_tok = [dq_done] * NSL
        gsl_dma = [None] * NSL
        usl_dma = [None] * NSL
        wds_tok = [dq_done] * 8
        wds_dma = [None] * 8
        gu_ps = [None, None]
        dn_ps = [None, None]
        acc_tok = [None] * 16
        gidx = 0

        for hf in range(2):
            c0 = KB * hf
            g2 = p.e("gpsimd")
            p.wait("gpsimd", ln_done if hf == 0 else acc_tok[15])
            if hf == 0:
                p.wait("gpsimd", dq_done)
            xsem = "xh%d" % hf
            txh = 0
            for kc in range(16):
                txh = p.inc(g2 and g2.dma_start(
                    out=xh[:, KB * kc:KB * (kc + 1)],
                    in_=xln_st[128 * kc:128 * (kc + 1), c0:c0 + KB]), xsem, 16)
            xh_tok = (xsem, txh)
            p.wait("tensor", xh_tok)
            for nb in range(16):
                sy = p.e("sync")
                for nc_ in range(8):
                    p.wait("sync", wds_tok[nc_])
                    rows = 1024 * nb + 128 * nc_
                    tdw = p.inc(sy and sy.dma_start(
                        out=wds[nc_], in_=wdd[rows:rows + 128, :]), "dw%d" % nc_, 16)
                    wds_dma[nc_] = ("dw%d" % nc_, tdw)
                inter_tok = [None] * 8
                for j in range(8):
                    jj = 128 * (8 * nb + j)
                    su = gidx % NSL
                    gidx += 1
                    sy = p.e("sync")
                    p.wait("sync", gsl_tok[su])
                    src_ap = bass.AP(wgd.tensor, jj,
                                     [[16384, 128], [128 * 16384, 16], [1, 128]])
                    tg_ = p.inc(sy and sy.dma_start(out=gsl[su], in_=src_ap),
                                "gw%d" % su, 16)
                    gsl_dma[su] = ("gw%d" % su, tg_)
                    p.wait("sync", usl_tok[su])
                    src_ap = bass.AP(wud.tensor, jj,
                                     [[16384, 128], [128 * 16384, 16], [1, 128]])
                    tu_ = p.inc(sy and sy.dma_start(out=usl[su], in_=src_ap),
                                "uw%d" % su, 16)
                    usl_dma[su] = ("uw%d" % su, tu_)
                    gb = 4 * (j % 2)
                    t = p.e("tensor")
                    p.wait("tensor", gsl_dma[su])
                    p.wait("tensor", usl_dma[su])
                    p.wait("tensor", gu_ps[j % 2])
                    if j == 0:
                        p.wait("tensor", dn_ps[0])
                        p.wait("tensor", dn_ps[1])
                    last = None
                    for kc in range(16):
                        for tc in range(2):
                            last = t and t.matmul(
                                psum[gb + tc][:, 0:512],
                                gsl[su][:, 128 * kc:128 * (kc + 1)],
                                xh[:, KB * kc + 512 * tc:KB * kc + 512 * (tc + 1)],
                                start=(kc == 0), stop=(kc == 15))
                    for kc in range(16):
                        for tc in range(2):
                            last = t and t.matmul(
                                psum[gb + 2 + tc][:, 0:512],
                                usl[su][:, 128 * kc:128 * (kc + 1)],
                                xh[:, KB * kc + 512 * tc:KB * kc + 512 * (tc + 1)],
                                start=(kc == 0), stop=(kc == 15))
                    tdk = p.inc(last, "td", 1)
                    gsl_tok[su] = ("td", tdk)
                    usl_tok[su] = ("td", tdk)
                    s = p.e("scalar")
                    p.wait("scalar", ("td", tdk))
                    for tc in range(2):
                        _ = s and s.activation(g1s[:, 512 * tc:512 * (tc + 1)],
                                               psum[gb + tc][:, 0:512], AF.Square)
                    ts1 = p.dinc("scalar", "sd")
                    v = p.e("vector")
                    p.wait("vector", ("sd", ts1))
                    for tc in range(2):
                        _ = v and v.tensor_copy(g2s[:, 512 * tc:512 * (tc + 1)],
                                                psum[gb + tc][:, 0:512])
                    _ = v and v.tensor_scalar(g1s, g1s, 0.044715, 1.0, op0=ALU.mult,
                                              op1=ALU.add)
                    _ = v and v.tensor_mul(g1s, g1s, g2s)
                    tvv = p.dinc("vector", "vd")
                    p.wait("scalar", ("vd", tvv))
                    _ = s and s.activation(g1s, g1s, AF.Tanh, scale=0.7978845608028654)
                    ts2 = p.dinc("scalar", "sd")
                    p.wait("vector", ("sd", ts2))
                    _ = v and v.tensor_scalar(g1s, g1s, 0.5, 0.5, op0=ALU.mult,
                                              op1=ALU.add)
                    for tc in range(2):
                        _ = v and v.tensor_tensor(g2s[:, 512 * tc:512 * (tc + 1)],
                                                  g2s[:, 512 * tc:512 * (tc + 1)],
                                                  psum[gb + 2 + tc][:, 0:512],
                                                  op=ALU.mult)
                    _ = v and v.tensor_tensor(inter[:, KB * j:KB * (j + 1)], g2s, g1s,
                                              op=ALU.mult)
                    tin = p.dinc("vector", "vd")
                    inter_tok[j] = ("vd", tin)
                    gu_ps[j % 2] = ("vd", tin)
                for ht in range(16):
                    hb2 = 4 * (ht % 2)
                    t = p.e("tensor")
                    p.wait("tensor", dn_ps[ht % 2])
                    if ht == 0:
                        p.wait("tensor", inter_tok[7])
                        for nc_ in range(8):
                            p.wait("tensor", wds_dma[nc_])
                    last = None
                    for nc_ in range(8):
                        for tc in range(2):
                            last = t and t.matmul(
                                psum[hb2 + tc][:, 0:512],
                                wds[nc_][:, 128 * ht:128 * (ht + 1)],
                                inter[:, KB * nc_ + 512 * tc:KB * nc_ + 512 * (tc + 1)],
                                start=(nc_ == 0), stop=(nc_ == 7))
                    tdd = p.inc(last, "td", 1)
                    v = p.e("vector")
                    p.wait("vector", ("td", tdd))
                    for tc in range(2):
                        if nb == 0:
                            _ = v and v.tensor_copy(
                                acc[:, KB * ht + 512 * tc:KB * ht + 512 * (tc + 1)],
                                psum[hb2 + tc][:, 0:512])
                        else:
                            _ = v and v.tensor_tensor(
                                acc[:, KB * ht + 512 * tc:KB * ht + 512 * (tc + 1)],
                                acc[:, KB * ht + 512 * tc:KB * ht + 512 * (tc + 1)],
                                psum[hb2 + tc][:, 0:512], op=ALU.add)
                    tac = p.dinc("vector", "vd")
                    acc_tok[ht] = ("vd", tac)
                    dn_ps[ht % 2] = ("vd", tac)
                for nc_ in range(8):
                    wds_tok[nc_] = ("td", p.cnt["td"])
            xi_free = [None, None]
            ost_free = [None, None]
            for ht in range(16):
                b = ht % 2
                g3 = p.e("gpsimd")
                p.wait("gpsimd", xi_free[b])
                txi = p.inc(g3 and g3.dma_start(
                    out=xit[b], in_=xi[128 * ht:128 * (ht + 1), c0:c0 + KB]),
                    "fx%d" % b, 16)
                v = p.e("vector")
                p.wait("vector", acc_tok[ht])
                p.wait("vector", ("fx%d" % b, txi))
                p.wait("vector", ost_free[b])
                _ = v and v.tensor_add(ost[b], acc[:, KB * ht:KB * (ht + 1)], xit[b])
                tvv = p.dinc("vector", "vd")
                xi_free[b] = ("vd", tvv)
                g4 = p.e("gpsimd")
                p.wait("gpsimd", ("vd", tvv))
                tst = p.inc(g4 and g4.dma_start(
                    out=xo[128 * ht:128 * (ht + 1), c0:c0 + KB], in_=ost[b]),
                    "fo%d" % b, 16)
                ost_free[b] = ("fo%d" % b, tst)
                acc_tok[ht] = ("vd", tvv)

        p.wait("sync", [p.tok("fo0"), p.tok("fo1")])

    return emit


def build_launch4(dbg=False):
    nc = bass.Bass("TRN2", target_bir_lowering=False)
    xi = nc.dram_tensor("xi", [D, T], F32, kind="ExternalInput").ap()
    xo = nc.dram_tensor("xo", [D, T], F32, kind="ExternalOutput").ap()
    return build_engines(nc, make_launch4(nc, xi, xo, "", dbg))


def run_launch4(d, xs, dbg=False):
    nc = build_launch4(dbg)
    in_maps = [dict(xi=xs[c], gwp=d["gwp"], gsc=d["gsc"], uwp=d["uwp"],
                    usc=d["usc"], dwp=d["dwp"], dsc=d["dsc"])
               for c in range(len(xs))]
    res = run_bass_kernel_spmd(nc, in_maps, core_ids=list(range(len(xs))))
    return res.results



def build_fused():
    nc = bass.Bass("TRN2", target_bir_lowering=False)
    x0 = nc.dram_tensor("xi", [D, T], F32, kind="ExternalInput").ap()
    x1 = nc.dram_tensor("x1", [D, T], F32, kind="Internal").ap()
    x2 = nc.dram_tensor("x2", [D, T], F32, kind="Internal").ap()
    x3 = nc.dram_tensor("x3", [D, T], F32, kind="Internal").ap()
    out = nc.dram_tensor("xo", [D, T], F32, kind="ExternalOutput").ap()
    e1 = make_launch1(nc, x0, x1, "L1")
    e2 = make_launch2(nc, x1, x2, "L2")
    e3 = make_launch3(nc, x2, x3, "L3")
    e4 = make_launch4(nc, x3, out, "L4")

    def bar(p, toks):
        for e in ENGS:
            p.wait(e, toks)

    def emit(p, C, psum):
        e1(p, C, psum)
        bar(p, [p.tok("oS0"), p.tok("oS1")])
        e2(p, C, psum)
        bar(p, [p.tok("os0"), p.tok("os1")])
        e3(p, C, psum)
        bar(p, [p.tok("oS0"), p.tok("oS1")])
        e4(p, C, psum)

    return build_engines(nc, emit)


def fused_in_map(d):
    return {
        "L1wqa": d["wqa"], "L1qnw": d["qnw"], "L1wqb": d["wqb"], "L1wkr": d["wkr"],
        "L1wkva": d["wkva"], "L1kvnw": d["kvnw"], "L1wkvb": d["wkvb"],
        "L1wo0": d["wo0"], "L1cos4": d["cos4b"], "L1sin4": d["sin4b"],
        "L1mdiag": d["mdiag"], "L1m1s": d["m1s"],
        "L2w1t": d["w1t"], "L2b1c": d["b1c"], "L2w2t": d["w2t"], "L2b2c": d["b2c"],
        "L2w3t": d["w3t"], "L2b3c": d["b3c"],
        "L3wq1": d["wq1"], "L3wk1": d["wk1"], "L3wv1": d["wv1"], "L3wo1": d["wo1"],
        "L3mdiag": d["mdiag"], "L3m1s": d["m1s"],
        "L4gwp": d["gwp"], "L4gsc": d["gsc"], "L4uwp": d["uwp"], "L4usc": d["usc"],
        "L4dwp": d["dwp"], "L4dsc": d["dsc"],
    }



def kernel(**inputs):
    d = prep_host(inputs)
    xs = d["_xcores"]
    nc = build_fused()
    base = fused_in_map(d)
    in_maps = [dict(base, xi=xs[c]) for c in range(NCORE)]
    res = run_bass_kernel_spmd(nc, in_maps, core_ids=list(range(NCORE)))
    out = np.stack([r["xo"].T for r in res.results])
    return np.ascontiguousarray(out)


def kernel_unfused(**inputs):
    d = prep_host(inputs)
    xs = d["_xcores"]
    r1 = run_launch1(d, xs)
    x1 = [np.ascontiguousarray(r["xo"]) for r in r1]
    r2 = run_launch2(d, x1)
    x2 = [np.ascontiguousarray(r["xo"]) for r in r2]
    r3 = run_launch3(d, x2)
    x3 = [np.ascontiguousarray(r["xo"]) for r in r3]
    r4 = run_launch4(d, x3)
    out = np.stack([r["xo"].T for r in r4])
    return np.ascontiguousarray(out)
